# version 01
# baseline (speedup 1.0000x reference)
from contextlib import ExitStack

import numpy as np
import concourse.bass as bass
import concourse.mybir as mybir
from concourse.bass_utils import run_bass_kernel_spmd

F32 = mybir.dt.float32
BF16 = mybir.dt.bfloat16
FP8 = mybir.dt.float8e4
AF = mybir.ActivationFunctionType
ALU = mybir.AluOpType
AX = mybir.AxisListType.X

N_CORES = 8
T_FULL = 16384
TOK = T_FULL // N_CORES
D = 2048
H, KD, BS = 16, 128, 64
I_FF = 5632
NI = I_FF // 128
NH = D // 128
TA = 256
NT = TOK // TA
NCH = TA // 128
SCALE = float(KD) ** -0.5
C_SHIFT = 12.0
EPS_DEN = 1e-6
LN_EPS = 1e-5
Q224 = 224.0


def build_program(debug=False, stop_phase=None):
    nc = bass.Bass("TRN2", target_bir_lowering=False, debug=False)

    def din(name, shape, dtype=F32):
        return nc.dram_tensor(name, list(shape), dtype, kind="ExternalInput").ap()

    xT = din("xT", (D, TOK))
    xN = din("xN", (TOK, D))
    Wq = din("Wq", (D, D)); Wk = din("Wk", (D, D)); Wv = din("Wv", (D, D))
    Wo = din("Wo", (D, D))
    bo_in = din("bo", (1, D))
    ps_in = din("ps", (KD, H)); pb_in = din("pb", (KD, H))
    wg_in = din("wg", (I_FF, D)); wu_in = din("wu", (I_FF, D))
    wd_in = din("wd", (D, I_FF))
    mask_in = din("mask", (KD, N_CORES))
    id32_in = din("id32", (128, 128))
    id8_in = din("id8", (128, 128), FP8)

    y_out = nc.dram_tensor("y", [TOK, D], F32, kind="ExternalOutput").ap()

    def dscr(name, shape, dtype=F32):
        return nc.dram_tensor(name, list(shape), dtype, kind="Internal").ap()

    k_scr = dscr("k_scr", (D, TOK))
    v_scr = dscr("v_scr", (TOK, D))
    pk_scr = dscr("pk_scr", (TOK, D))
    q_scr = dscr("q_scr", (D, TOK))
    x1_scr = dscr("x1_scr", (TOK, D))
    qwg_n = dscr("qwg_n", (I_FF, D), BF16)
    qwu_n = dscr("qwu_n", (I_FF, D), BF16)
    qwd_n = dscr("qwd_n", (D, I_FF), BF16)
    qwg_t = dscr("qwg_t", (D, I_FF), FP8)
    qwu_t = dscr("qwu_t", (D, I_FF), FP8)
    qwd_t = dscr("qwd_t", (I_FF, D), FP8)
    cc_in = dscr("cc_in", (128, H * 129))
    cc_out = nc.dram_tensor("cc_out", [N_CORES * 128, H * 129], F32,
                            kind="Internal", addr_space="Shared").ap()

    dbg = {}
    if debug:
        for nm, shape in [("d_k", (D, TOK)), ("d_v", (TOK, D)),
                          ("d_pk", (TOK, D)), ("d_q", (D, TOK)),
                          ("d_dS", (128, H * 129)), ("d_x1", (TOK, D)),
                          ("d_hyb", (D, TOK)), ("d_t", (TOK, I_FF)),
                          ("d_E", (64, 1024)), ("d_rcp", (64, 32)),
                          ("d_hblk", (64, 2048)), ("d_loc", (64, 2048)),
                          ("d_S", (128, H * 129)), ("d_den", (64, 32)),
                          ("d_lraw", (64, 128)), ("d_pre", (TA, D)),
                          ("d_stat", (128, 16)), ("d_g", (TA, I_FF)),
                          ("d_wsg", (128, I_FF)), ("d_qx", (128, NH * 256)),
                          ("d_qw", (128, 1536))]:
            dbg[nm] = nc.dram_tensor(nm, list(shape), F32,
                                     kind="ExternalOutput").ap()

    streams = {k: [] for k in ("sync", "scalar", "vector", "tensor", "gpsimd")}
    checkpoints = {}

    def checkpoint(name):
        checkpoints[name] = {k: len(v) for k, v in streams.items()}

    ctx = ExitStack()

    class Trk:
        def __init__(self, name):
            self.sem = ctx.enter_context(nc.semaphore(name))
            self.n = 0

    sD = Trk("sD"); sG = Trk("sG"); sA = Trk("sA")
    sPE = Trk("sPE"); sV = Trk("sV"); sS = Trk("sS"); sP = Trk("sP")
    sCC = Trk("sCC"); sO = Trk("sO")
    DMAQ = {"sync": sD, "gpsimd": sG, "scalar": sA}
    ETRK = {"tensor": sPE, "vector": sV, "scalar": sS, "gpsimd": sP}
    ALL_ENG = ("sync", "scalar", "vector", "tensor", "gpsimd")

    def dma(q, out_ap, in_ap, transpose=False):
        t = DMAQ[q]
        t.n += 16
        if transpose:
            streams[q].append(lambda e, o=out_ap, i=in_ap:
                              e.dma_start(out=o, in_=i, transpose=True)
                              .then_inc(t.sem, 16))
        else:
            streams[q].append(lambda e, o=out_ap, i=in_ap:
                              e.dma_start(out=o, in_=i).then_inc(t.sem, 16))
        return t.n

    def dma_out(q, out_ap, in_ap):
        sO.n += 16
        streams[q].append(lambda e, o=out_ap, i=in_ap:
                          e.dma_start(out=o, in_=i).then_inc(sO.sem, 16))
        return sO.n

    def wait(eng, trk, val):
        if val > 0:
            streams[eng].append(lambda e, s=trk.sem, v=val: e.wait_ge(s, v))

    def op(eng, fn, inc=False):
        if inc:
            t = ETRK[eng]
            t.n += 1
            streams[eng].append(lambda e, f=fn, s=t.sem: f(e).then_inc(s, 1))
            return t.n
        streams[eng].append(fn)
        return None

    def mm(fn):
        streams["tensor"].append(fn)

    def mark(eng):
        t = ETRK[eng]
        t.n += 1
        streams[eng].append(lambda e, s=t.sem: e.drain().then_inc(s, 1))
        return t.n

    def barrier(engs=ALL_ENG):
        marks = [(ETRK[e], ETRK[e].n) for e in ("tensor", "vector", "scalar",
                                                "gpsimd")]
        dmarks = [(sD, sD.n), (sG, sG.n), (sA, sA.n), (sO, sO.n)]
        for w in engs:
            for t, v in marks:
                if w in ETRK and ETRK[w] is t:
                    continue
                wait(w, t, v)
            for t, v in dmarks:
                wait(w, t, v)

    def sbuf(name, shape, dtype=F32, stack=None):
        return (stack or ctx).enter_context(
            nc.sbuf_tensor(name, list(shape), dtype))

    ps_sb = sbuf("ps_sb", (KD, H))
    pb_sb = sbuf("pb_sb", (KD, H))
    pb1_sb = sbuf("pb1_sb", (KD, H))
    mask_sb = sbuf("mask_sb", (KD, N_CORES))
    id32_sb = sbuf("id32_sb", (128, 128))
    id8_sb = sbuf("id8_sb", (128, 128), FP8)
    bo_sb = sbuf("bo_sb", (1, D))
    ones_sb = sbuf("ones_sb", (128, 132))
    S_sb = sbuf("S_sb", (128, H * 129))
    dS_acc = sbuf("dS_acc", (128, H * 129))
    wsc_g = sbuf("wsc_g", (128, NI))
    wsc_u = sbuf("wsc_u", (128, NI))
    wsc_d = sbuf("wsc_d", (128, NH))
    stat = sbuf("stat", (128, 16))
    cC_sb = sbuf("cC_sb", (128, 1))
    ceps_sb = sbuf("ceps_sb", (128, 1))
    ln224_sb = sbuf("ln224_sb", (128, 1))

    pbk = [ctx.enter_context(nc.psum_tensor(f"pb{i}", [128, 512], F32))
           for i in range(8)]

    dma("sync", ps_sb[:], ps_in[:, :])
    dma("sync", pb_sb[:], pb_in[:, :])
    dma("sync", mask_sb[:], mask_in[:, :])
    dma("sync", id32_sb[:], id32_in[:, :])
    dma("sync", id8_sb[:], id8_in[:, :])
    v0 = dma("sync", bo_sb[:], bo_in[0:1, :])
    wait("vector", sD, v0)
    op("vector", lambda e: e.memset(ones_sb[:], 1.0))
    op("vector", lambda e: e.memset(cC_sb[:], -C_SHIFT))
    op("vector", lambda e: e.memset(ceps_sb[:], LN_EPS))
    op("vector", lambda e: e.memset(ln224_sb[:], float(np.log(Q224))))
    op("vector", lambda e: e.tensor_scalar_add(pb1_sb[:], pb_sb[:], 1.0))
    op("vector", lambda e: e.memset(S_sb[:], 0.0))
    op("vector", lambda e: e.memset(dS_acc[:], 0.0))
    mark("vector")

    checkpoint("pre")
    wprep = ExitStack()
    wrow = [sbuf(f"wrow{i}", (128, I_FF), F32, wprep) for i in range(2)]
    wq16 = [sbuf(f"wq16{i}", (128, I_FF), FP8, wprep) for i in range(2)]
    wamax = sbuf("wamax", (128, 4), F32, wprep)

    qw_prev_v = [0, 0]
    qw_prev_g = [0, 0]

    def quant_weight(w_in, q_out, wsc_tile, ntiles, ncols):
        prev_v = qw_prev_v
        prev_g = qw_prev_g
        for it in range(ntiles):
            b = it % 2
            if prev_v[b]:
                wait("sync", sV, prev_v[b])
            vld = dma("sync", wrow[b][:, 0:ncols],
                      w_in[it * 128:(it + 1) * 128, :])
            wait("vector", sD, vld)
            if prev_g[b]:
                wait("vector", sG, prev_g[b])
            mam = op("vector", lambda e, b=b, c=ncols: e.tensor_reduce(
                out=wamax[:, b:b + 1], in_=wrow[b][:, 0:c], axis=AX,
                op=ALU.max, apply_absolute_value=True), inc=True)
            wait("scalar", sV, mam)
            op("scalar", lambda e, b=b: e.activation(
                out=wamax[:, b + 2:b + 3], in_=wamax[:, b:b + 1], func=AF.Ln))
            msx = op("scalar", lambda e, b=b: e.activation(
                out=wamax[:, b + 2:b + 3], in_=wamax[:, b + 2:b + 3],
                func=AF.Exp, bias=ln224_sb[:, 0:1], scale=-1.0), inc=True)
            wait("vector", sS, msx)
            op("vector", lambda e, b=b, c=ncols: e.tensor_scalar(
                out=wq16[b][:, 0:c], in0=wrow[b][:, 0:c],
                scalar1=wamax[:, b + 2:b + 3], scalar2=None, op0=ALU.mult))
            mv = op("vector", lambda e, b=b, t=it, w=wsc_tile: e.tensor_scalar(
                out=w[:, t:t + 1], in0=wamax[:, b:b + 1],
                scalar1=1.0 / Q224, scalar2=None, op0=ALU.mult), inc=True)
            prev_v[b] = mv
            wait("gpsimd", sV, mv)
            prev_g[b] = dma("gpsimd", q_out[it * 128:(it + 1) * 128, :],
                            wq16[b][:, 0:ncols])

    quant_weight(wg_in, qwg_n, wsc_g, NI, D)
    quant_weight(wu_in, qwu_n, wsc_u, NI, D)
    quant_weight(wd_in, qwd_n, wsc_d, NH, I_FF)
    g_wq = sG.n
    mark("vector")
    wprep.close()

    checkpoint("wq")
    wtr = ExitStack()
    ttile = [sbuf(f"ttile{i}", (128, I_FF), BF16, wtr) for i in range(2)]
    t8 = [sbuf(f"t8_{i}", (128, I_FF), FP8, wtr) for i in range(2)]

    tw_prev_v = [0, 0]
    tw_prev_g = [0, 0]

    def transpose_weight(nat, trs, nchunks, ncols):
        prev_v = tw_prev_v
        prev_g = tw_prev_g
        for ch in range(nchunks):
            b = ch % 2
            if prev_v[b]:
                wait("sync", sV, prev_v[b])
            vld = dma("sync", ttile[b][:, 0:ncols],
                      nat[0:ncols, ch * 128:(ch + 1) * 128], transpose=True)
            wait("vector", sD, vld)
            if prev_g[b]:
                wait("vector", sG, prev_g[b])
            mv = op("vector", lambda e, b=b, c=ncols: e.tensor_copy(
                out=t8[b][:, 0:c], in_=ttile[b][:, 0:c]), inc=True)
            prev_v[b] = mv
            wait("gpsimd", sV, mv)
            prev_g[b] = dma("gpsimd", trs[ch * 128:(ch + 1) * 128, :],
                            t8[b][:, 0:ncols])

    wait("sync", sG, g_wq)
    transpose_weight(qwg_n, qwg_t, NH, I_FF)
    transpose_weight(qwu_n, qwu_t, NH, I_FF)
    transpose_weight(qwd_n, qwd_t, NI, D)
    g_wt = sG.n
    wtr.close()

    checkpoint("wt")
    phA = ExitStack()
    xT_sb = sbuf("xT_sb", (128, NH, TOK), F32, phA)
    for c in range(NH):
        dma("sync", xT_sb[:, c, :], xT[c * 128:(c + 1) * 128, :])
    v_xT = sD.n

    a1 = ExitStack()
    wv_sb = sbuf("wv_sb", (128, NH, 512), F32, a1)
    kv_ev = sbuf("kv_ev", (128, 6 * 512), F32, a1)
    wait("tensor", sD, v_xT)
    m_mm_prev = 0
    m_ev_prev = 0
    g_store_prev = [0]
    for ft in range(4):
        if m_mm_prev:
            wait("scalar", sPE, m_mm_prev)
        vw = 0
        for hc in range(NH):
            vw = dma("scalar", wv_sb[:, hc, :],
                     Wv[hc * 128:(hc + 1) * 128, ft * 512:(ft + 1) * 512])
        wait("tensor", sA, vw)
        for tg in range(3):
            tcs = list(range(tg * 6, min(16, tg * 6 + 6)))
            if m_ev_prev:
                wait("tensor", sV, m_ev_prev)
            for hc in range(NH):
                for j, t0 in enumerate(tcs):
                    mm(lambda e, hc=hc, j=j, t0=t0: e.matmul(
                        pbk[j][:, :],
                        xT_sb[:, hc, t0 * 128:(t0 + 1) * 128],
                        wv_sb[:, hc, :],
                        start=(hc == 0), stop=(hc == NH - 1)))
            m_mm_prev = mark("tensor")
            wait("vector", sPE, m_mm_prev)
            wait("vector", sG, g_store_prev[0])
            for j, t0 in enumerate(tcs):
                op("vector", lambda e, j=j: e.tensor_copy(
                    out=kv_ev[:, j * 512:(j + 1) * 512], in_=pbk[j][:, :]))
            m_ev_prev = mark("vector")
            wait("gpsimd", sV, m_ev_prev)
            for j, t0 in enumerate(tcs):
                g_store_prev[0] = dma(
                    "gpsimd", v_scr[t0 * 128:(t0 + 1) * 128,
                                    ft * 512:(ft + 1) * 512],
                    kv_ev[:, j * 512:(j + 1) * 512])
    barrier()
    a1.close()
    if debug:
        dma_out("gpsimd", dbg["d_v"][:, :], v_scr[:, :])

    checkpoint("a2s")
    a2 = ExitStack()
    wh_sb = sbuf("wh_sb", (128, NH, 128), F32, a2)
    kfm_sb = sbuf("kfm_sb", (128, TOK), F32, a2)
    phk_sb = sbuf("phk_sb", (128, TOK), F32, a2)
    zp_sb = sbuf("zp_sb", (128, 512), F32, a2)
    mn_sb = sbuf("mn_sb", (128, 512), F32, a2)
    zi_sb = sbuf("zi_sb", (128, 512), mybir.dt.int32, a2)
    phkT_sb = sbuf("phkT_sb", (128, 256), F32, a2)
    vld_sb = sbuf("vld_sb", (128, 256), F32, a2)

    m_lastmm2 = [0]
    m_pkstore = [0]
    for hd in range(H):
        if hd > 0:
            wait("scalar", sPE, sPE.n)
        vw = 0
        for hc in range(NH):
            vw = dma("scalar", wh_sb[:, hc, :],
                     Wk[hc * 128:(hc + 1) * 128, hd * 128:(hd + 1) * 128])
        wait("tensor", sA, vw)
        wait("vector", sG, sG.n)
        m_ev = 0
        for tt in range(4):
            if m_ev:
                wait("tensor", sV, m_ev)
            for hc in range(NH):
                mm(lambda e, hc=hc, tt=tt: e.matmul(
                    pbk[tt % 2][:, :], wh_sb[:, hc, :],
                    xT_sb[:, hc, tt * 512:(tt + 1) * 512],
                    start=(hc == 0), stop=(hc == NH - 1)))
            mpe = mark("tensor")
            wait("vector", sPE, mpe)
            m_ev = op("vector", lambda e, tt=tt: e.tensor_copy(
                out=kfm_sb[:, tt * 512:(tt + 1) * 512], in_=pbk[tt % 2][:, :]),
                inc=True)
        m_k = m_ev
        wait("gpsimd", sV, m_k)
        dma("gpsimd", k_scr[hd * 128:(hd + 1) * 128, :], kfm_sb[:, :])
        wait("scalar", sV, m_k)
        m_phi = 0
        for sp in range(4):
            s0 = sp * 512
            msx = op("scalar", lambda e, s0=s0, hd=hd: e.activation(
                out=phk_sb[:, s0:s0 + 512], in_=kfm_sb[:, s0:s0 + 512],
                func=AF.Exp, bias=pb_sb[:, hd:hd + 1],
                scale=ps_sb[:, hd:hd + 1]), inc=True)
            wait("vector", sS, msx)
            op("vector", lambda e, s0=s0, hd=hd: e.tensor_scalar(
                out=zp_sb[:, :], in0=kfm_sb[:, s0:s0 + 512],
                scalar1=ps_sb[:, hd:hd + 1], scalar2=pb1_sb[:, hd:hd + 1],
                op0=ALU.mult, op1=ALU.add))
            op("vector", lambda e: e.tensor_scalar(
                out=mn_sb[:, :], in0=zp_sb[:, :], scalar1=10.0,
                scalar2=None, op0=ALU.min))
            op("vector", lambda e: e.tensor_scalar(
                out=zi_sb[:, :], in0=zp_sb[:, :], scalar1=1.0, scalar2=None,
                op0=ALU.is_gt))
            m_phi = op("vector", lambda e, s0=s0: e.select(
                out=phk_sb[:, s0:s0 + 512], mask=zi_sb[:, :],
                on_true=mn_sb[:, :], on_false=phk_sb[:, s0:s0 + 512]),
                inc=True)
            wait("scalar", sV, m_phi)
        wait("tensor", sV, m_phi)
        for tp in range(8):
            mpe_t = 0
            for cc in range(2):
                t0 = tp * 2 + cc
                mpe_t = op("tensor", lambda e, t0=t0, cc=cc: e.transpose(
                    pbk[2][:, cc * 128:(cc + 1) * 128],
                    phk_sb[:, t0 * 128:(t0 + 1) * 128], id32_sb[:]), inc=True)
            wait("vector", sPE, mpe_t)
            wait("vector", sG, m_pkstore[0])
            mv = op("vector", lambda e: e.tensor_copy(
                out=phkT_sb[:, :], in_=pbk[2][:, 0:256]), inc=True)
            wait("gpsimd", sV, mv)
            m_pkstore[0] = dma(
                "gpsimd",
                pk_scr[tp * 256:(tp + 1) * 256, hd * 128:(hd + 1) * 128]
                .rearrange("(c p) f -> p c f", c=2),
                phkT_sb[:, :].rearrange("p (c f) -> p c f", c=2))
            wait("sync", sPE, m_lastmm2[0])
            vld = dma("sync",
                      vld_sb[:, :].rearrange("p (c f) -> p c f", c=2),
                      v_scr[tp * 256:(tp + 1) * 256, hd * 128:(hd + 1) * 128]
                      .rearrange("(c p) f -> p c f", c=2))
            wait("tensor", sV, mv)
            wait("tensor", sD, vld)
            for cc in range(2):
                mm(lambda e, cc=cc: e.matmul(
                    pbk[3][:, 0:128], phkT_sb[:, cc * 128:(cc + 1) * 128],
                    vld_sb[:, cc * 128:(cc + 1) * 128],
                    start=True, stop=False))
                mpe2 = op("tensor", lambda e, cc=cc: e.matmul(
                    pbk[3][:, 128:129], phkT_sb[:, cc * 128:(cc + 1) * 128],
                    ones_sb[:, 0:1], start=False, stop=True), inc=True)
                m_lastmm2[0] = mpe2
                wait("vector", sPE, mpe2)
                mva = op("vector", lambda e, hd=hd: e.tensor_tensor(
                    out=dS_acc[:, hd * 129:(hd + 1) * 129],
                    in0=dS_acc[:, hd * 129:(hd + 1) * 129],
                    in1=pbk[3][:, 0:129], op=ALU.add), inc=True)
                wait("tensor", sV, mva)
    m_A = mark("vector")
    if debug:
        wait("gpsimd", sV, m_A)
        dma_out("gpsimd", dbg["d_dS"][:, :], dS_acc[:, :])
        dma_out("gpsimd", dbg["d_k"][:, :], k_scr[:, :])
        dma_out("gpsimd", dbg["d_pk"][:, :], pk_scr[:, :])
    barrier()
    a2.close()

    checkpoint("b0s")
    b0 = ExitStack()
    wqh_sb = sbuf("wqh_sb", (128, NH, 128), F32, b0)
    qfm_sb = sbuf("qfm_sb", (128, TOK), F32, b0)
    for hd in range(H):
        if hd > 0:
            wait("scalar", sPE, sPE.n)
        vw = 0
        for hc in range(NH):
            vw = dma("scalar", wqh_sb[:, hc, :],
                     Wq[hc * 128:(hc + 1) * 128, hd * 128:(hd + 1) * 128])
        wait("tensor", sA, vw)
        wait("vector", sG, sG.n)
        m_ev = 0
        for tt in range(4):
            if m_ev:
                wait("tensor", sV, m_ev)
            for hc in range(NH):
                mm(lambda e, hc=hc, tt=tt: e.matmul(
                    pbk[4 + tt % 2][:, :], wqh_sb[:, hc, :],
                    xT_sb[:, hc, tt * 512:(tt + 1) * 512],
                    start=(hc == 0), stop=(hc == NH - 1)))
            mpe = mark("tensor")
            wait("vector", sPE, mpe)
            m_ev = op("vector", lambda e, tt=tt: e.tensor_copy(
                out=qfm_sb[:, tt * 512:(tt + 1) * 512],
                in_=pbk[4 + tt % 2][:, :]), inc=True)
        wait("gpsimd", sV, m_ev)
        dma("gpsimd", q_scr[hd * 128:(hd + 1) * 128, :], qfm_sb[:, :])
    if debug:
        dma_out("gpsimd", dbg["d_q"][:, :], q_scr[:, :])
    barrier()
    b0.close()
    phA.close()

    checkpoint("b0")
    exch = ExitStack()
    gth = sbuf("gth", (128, H * 129), F32, exch)
    dma("gpsimd", cc_in[:, :], dS_acc[:, :])
    v_ccin = sG.n
    streams["gpsimd"].append(lambda e, v=v_ccin: e.wait_ge(sG.sem, v))
    sCC.n += 1
    streams["gpsimd"].append(lambda e: e.collective_compute(
        "AllGather", ALU.bypass, ins=[cc_in[:, :]], outs=[cc_out[:, :]],
        replica_groups=[list(range(N_CORES))]).then_inc(sCC.sem, 1))
    wait("sync", sCC, sCC.n)
    for r in range(N_CORES):
        vld = dma("sync", gth[:, :], cc_out[r * 128:(r + 1) * 128, :])
        wait("vector", sD, vld)
        m_acc = op("vector", lambda e, r=r: e.scalar_tensor_tensor(
            out=S_sb[:], in0=gth[:, :], scalar=mask_sb[:, r:r + 1],
            in1=S_sb[:], op0=ALU.mult, op1=ALU.add), inc=True)
        if r + 1 < N_CORES:
            wait("sync", sV, m_acc)
    barrier()
    exch.close()

    checkpoint("exch")
    ba = ExitStack()
    q_sb = sbuf("q_sb", (128, H, TA), F32, ba)
    k_sb = sbuf("k_sb", (128, H, TA), F32, ba)
    pq_sb = sbuf("pq_sb", (128, H, TA), F32, ba)
    zpb_sb = sbuf("zpb_sb", (128, TA), F32, ba)
    mnb_sb = sbuf("mnb_sb", (128, TA), F32, ba)
    zbi_sb = sbuf("zbi_sb", (128, TA), mybir.dt.int32, ba)
    pkT_sb = sbuf("pkT_sb", (64, 2, H * 128), F32, ba)
    va_sb = sbuf("va_sb", (64, 2, H, 129), F32, ba)
    E_sb = sbuf("E_sb", (64, H * 64), F32, ba)
    rcp_sb = sbuf("rcp_sb", (64, 3 * H), F32, ba)
    hyb_sb = sbuf("hyb_sb", (64, H * 128), F32, ba)
    hfm_sb = sbuf("hfm_sb", (128, H, TA), F32, ba)
    xn_sb = sbuf("xn_sb", (128, NCH, D), F32, ba)
    wo_sb = sbuf("wo_sb", (128, NH, 256), F32, ba)
    x1_sb = sbuf("x1_sb", (128, NCH, D), F32, ba)
    sq_sb = sbuf("sq_sb", (128, D), F32, ba)
    snap_sb = sbuf("snap_sb", (128, 16), F32, ba)

    op("vector", lambda e: e.memset(va_sb[:, :, :, 128:129], 1.0))
    mark("vector")

    for tt in range(NT):
        tb = tt * TA
        vq = vk = 0
        for hd in range(H):
            vq = dma("sync", q_sb[:, hd, :],
                     q_scr[hd * 128:(hd + 1) * 128, tb:tb + TA])
            vk = dma("sync", k_sb[:, hd, :],
                     k_scr[hd * 128:(hd + 1) * 128, tb:tb + TA])
        vxn = 0
        for cch in range(NCH):
            vxn = dma("sync", xn_sb[:, cch, :],
                      xN[tb + cch * 128:tb + (cch + 1) * 128, :])
        wait("scalar", sD, vq)
        m_pq = 0
        for hd in range(H):
            msx = op("scalar", lambda e, hd=hd: e.activation(
                out=pq_sb[:, hd, :], in_=q_sb[:, hd, :], func=AF.Exp,
                bias=pb_sb[:, hd:hd + 1], scale=ps_sb[:, hd:hd + 1]), inc=True)
            wait("vector", sS, msx)
            op("vector", lambda e, hd=hd: e.tensor_scalar(
                out=zpb_sb[:, :], in0=q_sb[:, hd, :],
                scalar1=ps_sb[:, hd:hd + 1], scalar2=pb1_sb[:, hd:hd + 1],
                op0=ALU.mult, op1=ALU.add))
            op("vector", lambda e: e.tensor_scalar(
                out=mnb_sb[:, :], in0=zpb_sb[:, :], scalar1=10.0,
                scalar2=None, op0=ALU.min))
            op("vector", lambda e: e.tensor_scalar(
                out=zbi_sb[:, :], in0=zpb_sb[:, :], scalar1=1.0,
                scalar2=None, op0=ALU.is_gt))
            m_pq = op("vector", lambda e, hd=hd: e.select(
                out=pq_sb[:, hd, :], mask=zbi_sb[:, :],
                on_true=mnb_sb[:, :], on_false=pq_sb[:, hd, :]), inc=True)
            wait("scalar", sV, m_pq)

        for bp in range(2):
            if tt > 0 or bp > 0:
                wait("sync", sPE, sPE.n)
            vv = vp = 0
            for blk2 in range(2):
                r0 = tb + bp * 128 + blk2 * 64
                vv = dma("sync", va_sb[:, blk2, :, 0:128],
                         v_scr[r0:r0 + 64, :])
                vp = dma("sync", pkT_sb[:, blk2, :], pk_scr[r0:r0 + 64, :])
            for blk2 in range(2):
                b = bp * 2 + blk2
                ts = b * 64
                wait("tensor", sD, max(vq, vk, vv, vp))
                wait("tensor", sV, m_pq)
                for g in range(2):
                    for h8 in range(8):
                        hd = g * 8 + h8
                        mm(lambda e, hd=hd, h8=h8, ts=ts: e.matmul(
                            pbk[0][0:64, h8 * 64:(h8 + 1) * 64],
                            k_sb[:, hd, ts:ts + 64],
                            q_sb[:, hd, ts:ts + 64],
                            start=True, stop=True))
                    mpe = mark("tensor")
                    wait("scalar", sPE, mpe)
                    mE = op("scalar", lambda e, g=g: e.activation(
                        out=E_sb[:, g * 512:(g + 1) * 512],
                        in_=pbk[0][0:64, :], func=AF.Exp,
                        bias=cC_sb[0:64, 0:1], scale=SCALE), inc=True)
                    wait("tensor", sS, mE)
                for hd in range(H):
                    mm(lambda e, hd=hd: e.matmul(
                        pbk[1][0:64, hd:hd + 1],
                        E_sb[:, hd * 64:(hd + 1) * 64],
                        ones_sb[0:64, 0:1],
                        start=(hd == 0), stop=False))
                for hd in range(H):
                    bank = 4 + hd // 4
                    off = (hd % 4) * 128
                    mm(lambda e, hd=hd, bank=bank, off=off, blk2=blk2:
                       e.matmul(
                           pbk[bank][0:64, off:off + 128],
                           E_sb[:, hd * 64:(hd + 1) * 64],
                           va_sb[:, blk2, hd, 0:128], start=True, stop=True))
                mpe_loc = mark("tensor")
                wait("vector", sPE, mpe_loc)
                if debug and tt == 0 and bp == 0 and blk2 == 0:
                    mvd = op("vector", lambda e: e.tensor_copy(
                        out=mnb_sb[0:64, 0:128], in_=pbk[4][0:64, 0:128]),
                        inc=True)
                    wait("gpsimd", sV, mvd)
                    dma_out("gpsimd", dbg["d_lraw"][:, :], mnb_sb[0:64, 0:128])
                wait("scalar", sPE, mpe_loc)
                op("scalar", lambda e: e.activation(
                    out=rcp_sb[:, 2 * H:3 * H], in_=pbk[1][0:64, 0:H],
                    func=AF.Ln))
                msm = op("scalar", lambda e: e.activation(
                    out=rcp_sb[:, 0:H], in_=rcp_sb[:, 2 * H:3 * H],
                    func=AF.Exp, scale=-1.0), inc=True)
                wait("vector", sS, msm)
                for hd in range(H):
                    bank = 4 + hd // 4
                    off = (hd % 4) * 128
                    op("vector", lambda e, hd=hd, bank=bank, off=off:
                       e.tensor_scalar(
                           out=hyb_sb[:, hd * 128:(hd + 1) * 128],
                           in0=pbk[bank][0:64, off:off + 128],
                           scalar1=rcp_sb[:, hd:hd + 1], scalar2=None,
                           op0=ALU.mult))
                m_loc = mark("vector")
                if debug and tt == 0 and bp == 0 and blk2 == 0:
                    wait("gpsimd", sV, m_loc)
                    dma_out("gpsimd", dbg["d_loc"][:, :], hyb_sb[:, :])
                wait("tensor", sV, m_loc)
                for hd in range(H):
                    bk = 2 + hd % 2
                    mm(lambda e, hd=hd, bk=bk, blk2=blk2: e.matmul(
                        pbk[bk][:, 0:128],
                        pkT_sb[:, blk2, hd * 128:(hd + 1) * 128],
                        va_sb[:, blk2, hd, 0:128], start=True, stop=False))
                    mpe = op("tensor", lambda e, hd=hd, bk=bk, blk2=blk2:
                             e.matmul(
                                 pbk[bk][:, 128:129],
                                 pkT_sb[:, blk2, hd * 128:(hd + 1) * 128],
                                 ones_sb[0:64, 0:1], start=False, stop=True),
                             inc=True)
                    wait("vector", sPE, mpe)
                    m_Sup = op("vector", lambda e, hd=hd, bk=bk:
                               e.tensor_tensor(
                                   out=S_sb[:, hd * 129:(hd + 1) * 129],
                                   in0=S_sb[:, hd * 129:(hd + 1) * 129],
                                   in1=pbk[bk][:, 0:129], op=ALU.add),
                               inc=True)
                    wait("tensor", sV, m_Sup)
                    nbank = 4 + hd // 4
                    noff = (hd % 4) * 128
                    mm(lambda e, hd=hd, nbank=nbank, noff=noff, ts=ts:
                       e.matmul(
                           pbk[nbank][0:64, noff:noff + 128],
                           pq_sb[:, hd, ts:ts + 64],
                           S_sb[:, hd * 129:hd * 129 + 128],
                           start=True, stop=True))
                    mm(lambda e, hd=hd, ts=ts: e.matmul(
                        pbk[1][0:64, H + hd:H + hd + 1],
                        pq_sb[:, hd, ts:ts + 64],
                        S_sb[:, hd * 129 + 128:hd * 129 + 129],
                        start=False, stop=(hd == H - 1)))
                mpe_num = mark("tensor")
                wait("vector", sPE, mpe_num)
                if debug and tt == 0 and bp == 0 and blk2 == 0:
                    mvd = op("vector", lambda e: e.tensor_copy(
                        out=mnb_sb[0:64, 0:32], in_=pbk[1][0:64, 0:32]),
                        inc=True)
                    wait("gpsimd", sV, mvd)
                    dma_out("gpsimd", dbg["d_den"][:, :], mnb_sb[0:64, 0:32])
                mgl = op("vector", lambda e: e.tensor_scalar(
                    out=rcp_sb[:, 2 * H:3 * H], in0=pbk[1][0:64, H:2 * H],
                    scalar1=EPS_DEN, scalar2=None, op0=ALU.max), inc=True)
                wait("scalar", sV, mgl)
                op("scalar", lambda e: e.activation(
                    out=rcp_sb[:, 2 * H:3 * H], in_=rcp_sb[:, 2 * H:3 * H],
                    func=AF.Ln))
                mgl2 = op("scalar", lambda e: e.activation(
                    out=rcp_sb[:, H:2 * H], in_=rcp_sb[:, 2 * H:3 * H],
                    func=AF.Exp, scale=-1.0), inc=True)
                wait("vector", sS, mgl2)
                for hd in range(H):
                    nbank = 4 + hd // 4
                    noff = (hd % 4) * 128
                    op("vector", lambda e, hd=hd, nbank=nbank, noff=noff:
                       e.scalar_tensor_tensor(
                           out=hyb_sb[:, hd * 128:(hd + 1) * 128],
                           in0=pbk[nbank][0:64, noff:noff + 128],
                           scalar=rcp_sb[:, H + hd:H + hd + 1],
                           in1=hyb_sb[:, hd * 128:(hd + 1) * 128],
                           op0=ALU.mult, op1=ALU.add))
                m_hyb = mark("vector")
                if debug and tt == 0 and bp == 0 and blk2 == 0:
                    wait("gpsimd", sV, m_hyb)
                    dma_out("gpsimd", dbg["d_S"][:, :], S_sb[:, :])
                    dma_out("gpsimd", dbg["d_E"][:, :], E_sb[:, :])
                    dma_out("gpsimd", dbg["d_rcp"][:, :], rcp_sb[:, 16:48])
                    dma_out("gpsimd", dbg["d_hblk"][:, :], hyb_sb[:, :])
                wait("tensor", sV, m_hyb)
                for hd in range(H):
                    mpe = op("tensor", lambda e, hd=hd: e.transpose(
                        pbk[2][:, 256:320],
                        hyb_sb[:, hd * 128:(hd + 1) * 128],
                        id32_sb[0:64, 0:64]), inc=True)
                    wait("vector", sPE, mpe)
                    mv = op("vector", lambda e, hd=hd, ts=ts: e.tensor_copy(
                        out=hfm_sb[:, hd, ts:ts + 64],
                        in_=pbk[2][:, 256:320]), inc=True)
                    wait("tensor", sV, mv)
        m_hfm = mark("vector")
        if debug:
            wait("gpsimd", sV, m_hfm)
            for hd in range(H):
                dma_out("gpsimd",
                        dbg["d_hyb"][hd * 128:(hd + 1) * 128, tb:tb + TA],
                        hfm_sb[:, hd, :])
        wait("tensor", sV, m_hfm)
        for htile in range(4):
            for half in range(2):
                c0 = htile * 512 + half * 256
                if htile > 0 or half > 0 or tt > 0:
                    wait("scalar", sPE, sPE.n)
                vwo = 0
                for hc in range(NH):
                    vwo = dma("scalar", wo_sb[:, hc, :],
                              Wo[hc * 128:(hc + 1) * 128, c0:c0 + 256])
                wait("tensor", sA, vwo)
                if htile == 0 and half == 0:
                    wait("tensor", sD, vxn)
                for cch in range(NCH):
                    for hc in range(NH):
                        mm(lambda e, hc=hc, cch=cch: e.matmul(
                            pbk[cch][:, 0:256],
                            hfm_sb[:, hc, cch * 128:(cch + 1) * 128],
                            wo_sb[:, hc, :], start=(hc == 0), stop=False))
                    mm(lambda e, cch=cch, c0=c0: e.matmul(
                        pbk[cch][:, 0:256], ones_sb[0:1, 0:128],
                        bo_sb[0:1, c0:c0 + 256], start=False, stop=True))
                mpe = mark("tensor")
                wait("vector", sPE, mpe)
                wait("vector", sD, vxn)
                for cch in range(NCH):
                    op("vector", lambda e, cch=cch, c0=c0: e.tensor_tensor(
                        out=x1_sb[:, cch, c0:c0 + 256],
                        in0=pbk[cch][:, 0:256],
                        in1=xn_sb[:, cch, c0:c0 + 256], op=ALU.add))
                m_ox = mark("vector")
                wait("tensor", sV, m_ox)
        if debug and tt == 0:
            wait("gpsimd", sV, m_ox)
            for cch in range(NCH):
                dma_out("gpsimd", dbg["d_pre"][cch * 128:(cch + 1) * 128, :],
                        x1_sb[:, cch, :])
            wait("vector", sO, sO.n)
            wait("scalar", sO, sO.n)
        wait("scalar", sV, m_ox)
        for cch in range(NCH):
            ms = op("scalar", lambda e, cch=cch: e.activation(
                out=sq_sb[:, :], in_=x1_sb[:, cch, :], func=AF.Square,
                accum_out=stat[:, 0:1]), inc=True)
            op("vector", lambda e, cch=cch: e.tensor_reduce(
                out=stat[:, 12:13], in_=x1_sb[:, cch, :], axis=AX, op=ALU.add))
            op("vector", lambda e: e.drain())
            wait("vector", sS, ms)
            op("vector", lambda e: e.tensor_scalar(
                out=stat[:, 12:13], in0=stat[:, 12:13], scalar1=1.0 / D,
                scalar2=None, op0=ALU.mult))
            op("vector", lambda e: e.tensor_scalar(
                out=stat[:, 0:1], in0=stat[:, 0:1], scalar1=1.0 / D,
                scalar2=None, op0=ALU.mult))
            op("vector", lambda e: e.drain())
            op("vector", lambda e: e.tensor_tensor(
                out=stat[:, 2:3], in0=stat[:, 12:13], in1=stat[:, 12:13],
                op=ALU.mult))
            op("vector", lambda e: e.drain())
            mv = op("vector", lambda e: e.tensor_tensor(
                out=stat[:, 2:3], in0=stat[:, 0:1], in1=stat[:, 2:3],
                op=ALU.subtract), inc=True)
            wait("scalar", sV, mv)
            op("scalar", lambda e: e.activation(
                out=stat[:, 3:4], in_=stat[:, 2:3], func=AF.Ln,
                bias=ceps_sb[:, 0:1], scale=1.0))
            ms2 = op("scalar", lambda e: e.activation(
                out=stat[:, 4:5], in_=stat[:, 3:4], func=AF.Exp,
                scale=-0.5), inc=True)
            wait("vector", sS, ms2)
            if debug and tt == 0 and cch == 1:
                op("vector", lambda e: e.tensor_copy(
                    out=snap_sb[:, 0:8], in_=x1_sb[:, 1, 0:8]))
                msnap = op("vector", lambda e: e.tensor_copy(
                    out=snap_sb[:, 8:16], in_=stat[:, 8:16]), inc=True)
                wait("gpsimd", sV, msnap)
                dma_out("gpsimd", dbg["d_stat"][:, :], snap_sb[:, :])
            m_x1 = op("vector", lambda e, cch=cch: e.tensor_scalar(
                out=x1_sb[:, cch, :], in0=x1_sb[:, cch, :],
                scalar1=stat[:, 12:13], scalar2=stat[:, 4:5],
                op0=ALU.subtract, op1=ALU.mult), inc=True)
            wait("gpsimd", sV, m_x1)
            dma("gpsimd", x1_scr[tb + cch * 128:tb + (cch + 1) * 128, :],
                x1_sb[:, cch, :])
            if debug:
                dma_out("gpsimd",
                        dbg["d_x1"][tb + cch * 128:tb + (cch + 1) * 128, :],
                        x1_sb[:, cch, :])
        barrier()
    ba.close()

    checkpoint("battn")
    bf = ExitStack()
    x1f_sb = sbuf("x1f_sb", (128, NCH, D), F32, bf)
    qxT_sb = sbuf("qxT_sb", (128, NH, NCH * 128), FP8, bf)
    g_sb = sbuf("g_sb", (128, NCH, I_FF), F32, bf)
    qtT_sb = sbuf("qtT_sb", (128, NI, NCH * 128), FP8, bf)
    diag_sb = sbuf("diag_sb", (128, 128), F32, bf)
    wstr_sb = sbuf("wstr_sb", (128, 2, 1536), FP8, bf)
    wdstr_sb = sbuf("wdstr_sb", (128, 2, D), FP8, bf)
    utmp_sb = sbuf("utmp_sb", (128, 512), F32, bf)
    y_sb = sbuf("y_sb", (128, NCH, D), F32, bf)
    wsgb_sb = sbuf("wsgb_sb", (128, I_FF), F32, bf)
    wsub_sb = sbuf("wsub_sb", (128, I_FF), F32, bf)
    wsdb_sb = sbuf("wsdb_sb", (128, D), F32, bf)
    trs_sb = sbuf("trs_sb", (128, 128), F32, bf)
    sx_sb = sbuf("sx_sb", (128, 2 * NCH), F32, bf)

    def bcast_scales(wsc_tile, ntiles, bc_out):
        mpe = op("tensor", lambda e, n=ntiles: e.transpose(
            pbk[7][0:n, 0:128], wsc_tile[:, 0:n], id32_sb[:]), inc=True)
        wait("vector", sPE, mpe)
        mv = op("vector", lambda e, n=ntiles: e.tensor_copy(
            out=trs_sb[0:n, :], in_=pbk[7][0:n, 0:128]), inc=True)
        wait("sync", sV, mv)
        dma("sync",
            bc_out[0:1, 0:ntiles * 128].rearrange("p (a b) -> p a b", a=ntiles),
            trs_sb[0:ntiles, :])
        wait("tensor", sD, sD.n)
        wait("tensor", sV, mv)
        nblk = (ntiles * 128 + 511) // 512
        for cb in range(nblk):
            c0 = cb * 512
            cw = min(512, ntiles * 128 - c0)
            mpe = op("tensor", lambda e, c0=c0, cw=cw: e.matmul(
                pbk[7][:, 0:cw], ones_sb[0:1, 0:128],
                bc_out[0:1, c0:c0 + cw], start=True, stop=True), inc=True)
            wait("vector", sPE, mpe)
            mv = op("vector", lambda e, c0=c0, cw=cw, b=bc_out: e.tensor_copy(
                out=b[:, c0:c0 + cw], in_=pbk[7][:, 0:cw]), inc=True)
            wait("tensor", sV, mv)
        mp = mark("vector")
        wait("gpsimd", sV, mp)
        wait("scalar", sV, mp)
        return mp

    m_tmp = bcast_scales(wsc_g, NI, wsgb_sb)
    if debug:
        wait("gpsimd", sV, m_tmp)
        dma_out("gpsimd", dbg["d_wsg"][:, :], wsgb_sb[:, :])
        wait("vector", sO, sO.n)
    bcast_scales(wsc_u, NI, wsub_sb)
    m_bc = bcast_scales(wsc_d, NH, wsdb_sb)
    wait("sync", sV, m_bc)
    wait("sync", sG, g_wt)
    wait("scalar", sG, g_wt)

    FF_G = [(0, 3), (3, 3), (6, 3), (9, 2)]

    for tt in range(NT):
        tb = tt * TA
        vx1 = 0
        for cch in range(NCH):
            vx1 = dma("sync", x1f_sb[:, cch, :],
                      x1_scr[tb + cch * 128:tb + (cch + 1) * 128, :])
        wait("vector", sD, vx1)
        for cch in range(NCH):
            mam = op("vector", lambda e, cch=cch: e.tensor_reduce(
                out=stat[:, 8:9], in_=x1f_sb[:, cch, :], axis=AX,
                op=ALU.max, apply_absolute_value=True), inc=True)
            op("vector", lambda e: e.drain())
            wait("scalar", sV, mam)
            op("scalar", lambda e: e.activation(
                out=stat[:, 9:10], in_=stat[:, 8:9], func=AF.Ln))
            msx = op("scalar", lambda e: e.activation(
                out=stat[:, 9:10], in_=stat[:, 9:10], func=AF.Exp,
                bias=ln224_sb[:, 0:1], scale=-1.0), inc=True)
            op("vector", lambda e, cch=cch: e.tensor_scalar(
                out=sx_sb[:, cch:cch + 1], in0=stat[:, 8:9],
                scalar1=1.0 / Q224, scalar2=None, op0=ALU.mult))
            wait("vector", sS, msx)
            m_dg = op("vector", lambda e: e.tensor_scalar(
                out=diag_sb[:, :], in0=id32_sb[:, :],
                scalar1=stat[:, 9:10], scalar2=None, op0=ALU.mult), inc=True)
            wait("tensor", sV, m_dg)
            for hc in range(NH):
                mpe = op("tensor", lambda e, cch=cch, hc=hc: e.matmul(
                    pbk[6][:, 0:128],
                    x1f_sb[:, cch, hc * 128:(hc + 1) * 128],
                    diag_sb[:, :], start=True, stop=True), inc=True)
                wait("vector", sPE, mpe)
                mv = op("vector", lambda e, cch=cch, hc=hc: e.tensor_copy(
                    out=qxT_sb[:, hc, cch * 128:(cch + 1) * 128],
                    in_=pbk[6][:, 0:128]), inc=True)
                wait("tensor", sV, mv)
        m_qxT = mark("vector")
        if debug and tt == 0:
            for hc in range(NH):
                mqd = op("vector", lambda e, hc=hc: e.tensor_copy(
                    out=y_sb[:, 0, 0:256], in_=qxT_sb[:, hc, :]), inc=True)
                wait("gpsimd", sV, mqd)
                dma_out("gpsimd", dbg["d_qx"][:, hc * 256:(hc + 1) * 256],
                        y_sb[:, 0, 0:256])
                wait("vector", sO, sO.n)
        wait("tensor", sV, m_qxT)

        pe_h = [0, 0]
        for wpass in range(2):
            w_t = qwg_t if wpass == 0 else qwu_t
            wsb = wsgb_sb if wpass == 0 else wsub_sb
            for (f0t, nft) in FF_G:
                f0 = f0t * 512
                nf = nft * 512
                for hc in range(NH):
                    bsl = hc % 2
                    if pe_h[bsl]:
                        wait("scalar", sPE, pe_h[bsl])
                    vw = dma("scalar", wstr_sb[:, bsl, 0:nf],
                             w_t[hc * 128:(hc + 1) * 128, f0:f0 + nf])
                    wait("tensor", sA, vw)

                    for cch in range(NCH):
                        for fl in range(nft):
                            mm(lambda e, hc=hc, cch=cch, fl=fl, bsl=bsl:
                               e.matmul(
                                   pbk[cch * 3 + fl][:, :],
                                   qxT_sb[:, hc, cch * 128:(cch + 1) * 128],
                                   wstr_sb[:, bsl, fl * 512:(fl + 1) * 512],
                                   start=(hc == 0), stop=(hc == NH - 1)))
                    pe_h[bsl] = mark("tensor")
                mpe_gi = pe_h[(NH - 1) % 2]
                wait("vector", sPE, mpe_gi)
                if debug and tt == 0 and wpass == 0 and f0 == 0:
                    mqd = op("vector", lambda e: e.tensor_copy(
                        out=utmp_sb[:, :], in_=pbk[0][:, :]), inc=True)
                    wait("gpsimd", sV, mqd)
                    dma_out("gpsimd", dbg["d_qw"][:, 0:512], utmp_sb[:, :])
                    wait("vector", sO, sO.n)
                for cch in range(NCH):
                    for fl in range(nft):
                        fo = f0 + fl * 512
                        if wpass == 0:
                            op("vector", lambda e, cch=cch, fl=fl, fo=fo,
                               wsb=wsb: e.scalar_tensor_tensor(
                                   out=g_sb[:, cch, fo:fo + 512],
                                   in0=pbk[cch * 3 + fl][:, :],
                                   scalar=sx_sb[:, cch:cch + 1],
                                   in1=wsb[:, fo:fo + 512],
                                   op0=ALU.mult, op1=ALU.mult))
                        else:
                            op("vector", lambda e, cch=cch, fl=fl, fo=fo,
                               wsb=wsb: e.scalar_tensor_tensor(
                                   out=utmp_sb[:, :],
                                   in0=pbk[cch * 3 + fl][:, :],
                                   scalar=sx_sb[:, cch:cch + 1],
                                   in1=wsb[:, fo:fo + 512],
                                   op0=ALU.mult, op1=ALU.mult))
                            op("vector", lambda e, cch=cch, fo=fo:
                               e.tensor_tensor(
                                   out=g_sb[:, cch, fo:fo + 512],
                                   in0=g_sb[:, cch, fo:fo + 512],
                                   in1=utmp_sb[:, :], op=ALU.mult))
                m_ev = mark("vector")
                wait("tensor", sV, m_ev)
            if wpass == 0:
                m_g = mark("vector")
                if debug and tt == 0:
                    wait("gpsimd", sV, m_g)
                    for cch in range(NCH):
                        dma_out("gpsimd",
                                dbg["d_g"][cch * 128:(cch + 1) * 128, :],
                                g_sb[:, cch, :])
                    wait("scalar", sO, sO.n)
                wait("scalar", sV, m_g)
                msl = 0
                for cch in range(NCH):
                    msl = op("scalar", lambda e, cch=cch: e.activation(
                        out=g_sb[:, cch, :], in_=g_sb[:, cch, :],
                        func=AF.Silu), inc=True)
                wait("vector", sS, msl)
        if debug:
            m_t = mark("vector")
            wait("gpsimd", sV, m_t)
            for cch in range(NCH):
                dma_out("gpsimd",
                        dbg["d_t"][tb + cch * 128:tb + (cch + 1) * 128, :],
                        g_sb[:, cch, :])
        for cch in range(NCH):
            mam = op("vector", lambda e, cch=cch: e.tensor_reduce(
                out=stat[:, 10:11], in_=g_sb[:, cch, :], axis=AX,
                op=ALU.max, apply_absolute_value=True), inc=True)
            op("vector", lambda e: e.drain())
            wait("scalar", sV, mam)
            op("scalar", lambda e: e.activation(
                out=stat[:, 11:12], in_=stat[:, 10:11], func=AF.Ln))
            msx = op("scalar", lambda e: e.activation(
                out=stat[:, 11:12], in_=stat[:, 11:12], func=AF.Exp,
                bias=ln224_sb[:, 0:1], scale=-1.0), inc=True)
            op("vector", lambda e, cch=cch: e.tensor_scalar(
                out=sx_sb[:, NCH + cch:NCH + cch + 1], in0=stat[:, 10:11],
                scalar1=1.0 / Q224, scalar2=None, op0=ALU.mult))
            wait("vector", sS, msx)
            m_dg = op("vector", lambda e: e.tensor_scalar(
                out=diag_sb[:, :], in0=id32_sb[:, :],
                scalar1=stat[:, 11:12], scalar2=None, op0=ALU.mult), inc=True)
            wait("tensor", sV, m_dg)
            for ic in range(NI):
                mpe = op("tensor", lambda e, cch=cch, ic=ic: e.matmul(
                    pbk[6][:, 0:128],
                    g_sb[:, cch, ic * 128:(ic + 1) * 128],
                    diag_sb[:, :], start=True, stop=True), inc=True)
                wait("vector", sPE, mpe)
                mv = op("vector", lambda e, cch=cch, ic=ic: e.tensor_copy(
                    out=qtT_sb[:, ic, cch * 128:(cch + 1) * 128],
                    in_=pbk[6][:, 0:128]), inc=True)
                wait("tensor", sV, mv)
        m_qtT = mark("vector")
        wait("tensor", sV, m_qtT)
        pe_d = [0, 0]
        for ic in range(NI):
            bsl = ic % 2
            if pe_d[bsl]:
                wait("scalar", sPE, pe_d[bsl])
            vw = dma("scalar", wdstr_sb[:, bsl, :],
                     qwd_t[ic * 128:(ic + 1) * 128, :])
            wait("tensor", sA, vw)
            for cch in range(NCH):
                for ht in range(4):
                    mm(lambda e, ic=ic, cch=cch, ht=ht, bsl=bsl: e.matmul(
                        pbk[cch * 4 + ht][:, :],
                        qtT_sb[:, ic, cch * 128:(cch + 1) * 128],
                        wdstr_sb[:, bsl, ht * 512:(ht + 1) * 512],
                        start=(ic == 0), stop=(ic == NI - 1)))
            pe_d[bsl] = mark("tensor")
        mpe_d = pe_d[(NI - 1) % 2]
        wait("vector", sPE, mpe_d)
        for cch in range(NCH):
            for ht in range(4):
                op("vector", lambda e, cch=cch, ht=ht: e.scalar_tensor_tensor(
                    out=y_sb[:, cch, ht * 512:(ht + 1) * 512],
                    in0=pbk[cch * 4 + ht][:, :],
                    scalar=sx_sb[:, NCH + cch:NCH + cch + 1],
                    in1=wsdb_sb[:, ht * 512:(ht + 1) * 512],
                    op0=ALU.mult, op1=ALU.mult))
            op("vector", lambda e, cch=cch: e.tensor_tensor(
                out=y_sb[:, cch, :], in0=y_sb[:, cch, :],
                in1=x1f_sb[:, cch, :], op=ALU.add))
        m_y = mark("vector")
        wait("tensor", sV, m_y)
        wait("scalar", sV, m_y)
        for cch in range(NCH):
            ms = op("scalar", lambda e, cch=cch: e.activation(
                out=x1f_sb[:, 0, :], in_=y_sb[:, cch, :], func=AF.Square,
                accum_out=stat[:, 0:1]), inc=True)
            op("vector", lambda e, cch=cch: e.tensor_reduce(
                out=stat[:, 12:13], in_=y_sb[:, cch, :], axis=AX, op=ALU.add))
            op("vector", lambda e: e.drain())
            wait("vector", sS, ms)
            op("vector", lambda e: e.tensor_scalar(
                out=stat[:, 12:13], in0=stat[:, 12:13], scalar1=1.0 / D,
                scalar2=None, op0=ALU.mult))
            op("vector", lambda e: e.tensor_scalar(
                out=stat[:, 0:1], in0=stat[:, 0:1], scalar1=1.0 / D,
                scalar2=None, op0=ALU.mult))
            op("vector", lambda e: e.drain())
            op("vector", lambda e: e.tensor_tensor(
                out=stat[:, 2:3], in0=stat[:, 12:13], in1=stat[:, 12:13],
                op=ALU.mult))
            op("vector", lambda e: e.drain())
            mv = op("vector", lambda e: e.tensor_tensor(
                out=stat[:, 2:3], in0=stat[:, 0:1], in1=stat[:, 2:3],
                op=ALU.subtract), inc=True)
            wait("scalar", sV, mv)
            op("scalar", lambda e: e.activation(
                out=stat[:, 3:4], in_=stat[:, 2:3], func=AF.Ln,
                bias=ceps_sb[:, 0:1], scale=1.0))
            ms2 = op("scalar", lambda e: e.activation(
                out=stat[:, 4:5], in_=stat[:, 3:4], func=AF.Exp,
                scale=-0.5), inc=True)
            wait("vector", sS, ms2)
            m_fin = op("vector", lambda e, cch=cch: e.tensor_scalar(
                out=y_sb[:, cch, :], in0=y_sb[:, cch, :],
                scalar1=stat[:, 12:13], scalar2=stat[:, 4:5],
                op0=ALU.subtract, op1=ALU.mult), inc=True)
            wait("gpsimd", sV, m_fin)
            dma_out("gpsimd", y_out[tb + cch * 128:tb + (cch + 1) * 128, :],
                    y_sb[:, cch, :])
        barrier()
    bf.close()

    checkpoint("bffn")
    if stop_phase is not None:
        cut = checkpoints[stop_phase]
        for k in streams:
            streams[k] = streams[k][:cut[k]]
    else:
        streams["gpsimd"].append(lambda e, v=sO.n: e.wait_ge(sO.sem, v))

    with nc.Block() as block:
        block.sync(lambda e, s=streams["sync"]: [f(e) for f in s])
        block.scalar(lambda e, s=streams["scalar"]: [f(e) for f in s])
        block.vector(lambda e, s=streams["vector"]: [f(e) for f in s])
        block.tensor(lambda e, s=streams["tensor"]: [f(e) for f in s])
        block.gpsimd(lambda e, s=streams["gpsimd"]: [f(e) for f in s])

    ctx.close()
    return nc, dbg


_NC_CACHE = {}


def _get_nc(debug=False):
    if debug not in _NC_CACHE:
        _NC_CACHE[debug] = build_program(debug=debug)
    return _NC_CACHE[debug]


def make_in_maps(inputs):
    import ml_dtypes
    x = np.asarray(inputs["L0_attn_x"])[0]
    Wq_ = np.ascontiguousarray(np.asarray(inputs["L0_attn_Wq"], np.float32))
    Wk_ = np.ascontiguousarray(np.asarray(inputs["L0_attn_Wk"], np.float32))
    Wv_ = np.ascontiguousarray(np.asarray(inputs["L0_attn_Wv"], np.float32))
    Wo_ = np.ascontiguousarray(np.asarray(inputs["L0_attn_Wo"], np.float32))
    bo_ = np.asarray(inputs["L0_attn_bo"], np.float32).reshape(1, D)
    ps_ = np.ascontiguousarray(
        np.asarray(inputs["L0_attn_phi_scale"], np.float32)[:, 0, :].T)
    pb_ = np.ascontiguousarray(
        np.asarray(inputs["L0_attn_phi_bias"], np.float32)[:, 0, :].T)
    sigma = np.asarray(inputs["L0_ffn_down_reorder_index"]).astype(np.int64)
    wg_ = np.ascontiguousarray(
        np.asarray(inputs["L0_ffn_w_gate"], np.float32)[sigma])
    wu_ = np.ascontiguousarray(
        np.asarray(inputs["L0_ffn_w_up"], np.float32)[sigma])
    wd_ = np.ascontiguousarray(np.asarray(inputs["L0_ffn_w_down"], np.float32))
    id32 = np.eye(128, dtype=np.float32)
    id8 = np.eye(128, dtype=np.float32).astype(ml_dtypes.float8_e4m3)
    in_maps = []
    for c in range(N_CORES):
        sl = slice(c * TOK, (c + 1) * TOK)
        msk = np.zeros((KD, N_CORES), np.float32)
        msk[:, :c] = 1.0
        in_maps.append({
            "xT": np.ascontiguousarray(x[sl].T.astype(np.float32)),
            "xN": np.ascontiguousarray(x[sl].astype(np.float32)),
            "Wq": Wq_, "Wk": Wk_, "Wv": Wv_, "Wo": Wo_, "bo": bo_,
            "ps": ps_, "pb": pb_, "wg": wg_, "wu": wu_, "wd": wd_,
            "mask": msk, "id32": id32, "id8": id8,
        })
    return in_maps


def kernel(**inputs):
    nc, _ = _get_nc(debug=False)
    in_maps = make_in_maps(inputs)
    res = run_bass_kernel_spmd(nc, in_maps, core_ids=list(range(N_CORES)))
    y = np.concatenate([r["y"] for r in res.results], axis=0)
    return y.reshape(1, T_FULL, D).astype(np.float32)


if __name__ == "__main__":
    nc, _ = build_program(debug=False)
    n = sum(len(b.instructions) if hasattr(b, "instructions") else 0
            for b in []) or "?"
    print("built ok")


# version 07 (final)
# speedup vs baseline: 1.1135x; 1.1135x over previous
from contextlib import ExitStack

import numpy as np
import concourse.bass as bass
import concourse.mybir as mybir
from concourse.bass_utils import run_bass_kernel_spmd

F32 = mybir.dt.float32
BF16 = mybir.dt.bfloat16
FP8 = mybir.dt.float8e4
AF = mybir.ActivationFunctionType
ALU = mybir.AluOpType
AX = mybir.AxisListType.X

N_CORES = 8
T_FULL = 16384
TOK = T_FULL // N_CORES
D = 2048
H, KD, BS = 16, 128, 64
I_FF = 5632
NI = I_FF // 128
NH = D // 128
TA = 256
NT = TOK // TA
NCH = TA // 128
SCALE = float(KD) ** -0.5
C_SHIFT = 12.0
EPS_DEN = 1e-6
LN_EPS = 1e-5
Q224 = 224.0


def build_program(debug=False, stop_phase=None):
    nc = bass.Bass("TRN2", target_bir_lowering=False, debug=False)

    def din(name, shape, dtype=F32):
        return nc.dram_tensor(name, list(shape), dtype, kind="ExternalInput").ap()

    xT = din("xT", (D, TOK))
    xN = din("xN", (TOK, D))
    Wq = din("Wq", (D, D)); Wk = din("Wk", (D, D)); Wv = din("Wv", (D, D))
    Wo = din("Wo", (D, D))
    bo_in = din("bo", (1, D))
    ps_in = din("ps", (KD, H)); pb_in = din("pb", (KD, H))
    wg_in = din("wg", (I_FF, D)); wu_in = din("wu", (I_FF, D))
    wd_in = din("wd", (D, I_FF))
    mask_in = din("mask", (KD, N_CORES))
    id32_in = din("id32", (128, 128))
    id8_in = din("id8", (128, 128), FP8)

    y_out = nc.dram_tensor("y", [TOK, D], F32, kind="ExternalOutput").ap()

    def dscr(name, shape, dtype=F32):
        return nc.dram_tensor(name, list(shape), dtype, kind="Internal").ap()

    k_scr = dscr("k_scr", (D, TOK))
    v_scr = dscr("v_scr", (TOK, D))
    pk_scr = dscr("pk_scr", (TOK, D))
    q_scr = dscr("q_scr", (D, TOK))
    x1_scr = dscr("x1_scr", (TOK, D))
    qwg_n = dscr("qwg_n", (I_FF, D), BF16)
    qwu_n = dscr("qwu_n", (I_FF, D), BF16)
    qwd_n = dscr("qwd_n", (D, I_FF), BF16)
    qwg_t = dscr("qwg_t", (D, I_FF), FP8)
    qwu_t = dscr("qwu_t", (D, I_FF), FP8)
    qwd_t = dscr("qwd_t", (I_FF, D), FP8)
    cc_in = dscr("cc_in", (128, H * 129))
    cc_out = nc.dram_tensor("cc_out", [N_CORES * 128, H * 129], F32,
                            kind="Internal", addr_space="Shared").ap()

    dbg = {}
    if debug:
        for nm, shape in [("d_k", (D, TOK)), ("d_v", (TOK, D)),
                          ("d_pk", (TOK, D)), ("d_q", (D, TOK)),
                          ("d_dS", (128, H * 129)), ("d_x1", (TOK, D)),
                          ("d_hyb", (D, TOK)), ("d_t", (TOK, I_FF)),
                          ("d_E", (64, 1024)), ("d_rcp", (64, 32)),
                          ("d_hblk", (64, 2048)), ("d_loc", (64, 2048)),
                          ("d_S", (128, H * 129)), ("d_den", (64, 32)),
                          ("d_lraw", (64, 128)), ("d_pre", (TA, D)),
                          ("d_stat", (128, 16)), ("d_g", (TA, I_FF)),
                          ("d_wsg", (128, I_FF)), ("d_qx", (128, NH * 256)),
                          ("d_qw", (128, 1536))]:
            dbg[nm] = nc.dram_tensor(nm, list(shape), F32,
                                     kind="ExternalOutput").ap()

    streams = {k: [] for k in ("sync", "scalar", "vector", "tensor", "gpsimd")}
    checkpoints = {}

    def checkpoint(name):
        checkpoints[name] = {k: len(v) for k, v in streams.items()}

    ctx = ExitStack()

    class Trk:
        def __init__(self, name):
            self.sem = ctx.enter_context(nc.semaphore(name))
            self.n = 0

    sD = Trk("sD"); sG = Trk("sG"); sA = Trk("sA")
    sPE = Trk("sPE"); sV = Trk("sV"); sS = Trk("sS"); sP = Trk("sP")
    sCC = Trk("sCC"); sO = Trk("sO")
    DMAQ = {"sync": sD, "gpsimd": sG, "scalar": sA}
    ETRK = {"tensor": sPE, "vector": sV, "scalar": sS, "gpsimd": sP}
    ALL_ENG = ("sync", "scalar", "vector", "tensor", "gpsimd")

    def dma(q, out_ap, in_ap, transpose=False):
        t = DMAQ[q]
        t.n += 16
        if transpose:
            streams[q].append(lambda e, o=out_ap, i=in_ap:
                              e.dma_start(out=o, in_=i, transpose=True)
                              .then_inc(t.sem, 16))
        else:
            streams[q].append(lambda e, o=out_ap, i=in_ap:
                              e.dma_start(out=o, in_=i).then_inc(t.sem, 16))
        return t.n

    def dma_out(q, out_ap, in_ap):
        sO.n += 16
        streams[q].append(lambda e, o=out_ap, i=in_ap:
                          e.dma_start(out=o, in_=i).then_inc(sO.sem, 16))
        return sO.n

    def wait(eng, trk, val):
        if val > 0:
            streams[eng].append(lambda e, s=trk.sem, v=val: e.wait_ge(s, v))

    def op(eng, fn, inc=False):
        if inc:
            t = ETRK[eng]
            t.n += 1
            streams[eng].append(lambda e, f=fn, s=t.sem: f(e).then_inc(s, 1))
            return t.n
        streams[eng].append(fn)
        return None

    def mm(fn):
        streams["tensor"].append(fn)

    def mark(eng):
        t = ETRK[eng]
        t.n += 1
        streams[eng].append(lambda e, s=t.sem: e.drain().then_inc(s, 1))
        return t.n

    def barrier(engs=ALL_ENG):
        marks = [(ETRK[e], ETRK[e].n) for e in ("tensor", "vector", "scalar",
                                                "gpsimd")]
        dmarks = [(sD, sD.n), (sG, sG.n), (sA, sA.n), (sO, sO.n)]
        for w in engs:
            for t, v in marks:
                if w in ETRK and ETRK[w] is t:
                    continue
                wait(w, t, v)
            for t, v in dmarks:
                wait(w, t, v)

    def sbuf(name, shape, dtype=F32, stack=None):
        return (stack or ctx).enter_context(
            nc.sbuf_tensor(name, list(shape), dtype))

    ps_sb = sbuf("ps_sb", (KD, H))
    pb_sb = sbuf("pb_sb", (KD, H))
    pb1_sb = sbuf("pb1_sb", (KD, H))
    mask_sb = sbuf("mask_sb", (KD, N_CORES))
    id32_sb = sbuf("id32_sb", (128, 128))
    id8_sb = sbuf("id8_sb", (128, 128), FP8)
    bo_sb = sbuf("bo_sb", (1, D))
    ones_sb = sbuf("ones_sb", (128, 132))
    S_sb = sbuf("S_sb", (128, H * 129))
    dS_acc = sbuf("dS_acc", (128, H * 129))
    wsc_g = sbuf("wsc_g", (128, NI))
    wsc_u = sbuf("wsc_u", (128, NI))
    wsc_d = sbuf("wsc_d", (128, NH))
    stat = sbuf("stat", (128, 16))
    cC_sb = sbuf("cC_sb", (128, 1))
    ceps_sb = sbuf("ceps_sb", (128, 1))
    ln224_sb = sbuf("ln224_sb", (128, 1))

    pbk = [ctx.enter_context(nc.psum_tensor(f"pb{i}", [128, 512], F32))
           for i in range(8)]

    dma("sync", ps_sb[:], ps_in[:, :])
    dma("sync", pb_sb[:], pb_in[:, :])
    dma("sync", mask_sb[:], mask_in[:, :])
    dma("sync", id32_sb[:], id32_in[:, :])
    dma("sync", id8_sb[:], id8_in[:, :])
    v0 = dma("sync", bo_sb[:], bo_in[0:1, :])
    wait("vector", sD, v0)
    op("vector", lambda e: e.memset(ones_sb[:], 1.0))
    op("vector", lambda e: e.memset(cC_sb[:], -C_SHIFT))
    op("vector", lambda e: e.memset(ceps_sb[:], LN_EPS))
    op("vector", lambda e: e.memset(ln224_sb[:], float(np.log(Q224))))
    op("vector", lambda e: e.tensor_scalar_add(pb1_sb[:], pb_sb[:], 1.0))
    op("vector", lambda e: e.memset(S_sb[:], 0.0))
    op("vector", lambda e: e.memset(dS_acc[:], 0.0))
    mark("vector")

    checkpoint("pre")
    wprep = ExitStack()
    wrow = [sbuf(f"wrow{i}", (128, I_FF), F32, wprep) for i in range(2)]
    wq16 = [sbuf(f"wq16{i}", (128, I_FF), FP8, wprep) for i in range(2)]
    wamax = sbuf("wamax", (128, 4), F32, wprep)

    qw_prev_v = [0, 0]
    qw_prev_g = [0, 0]
    qw_prev_am = [0, 0]

    def quant_weight(w_in, q_out, wsc_tile, ntiles, ncols):
        prev_v = qw_prev_v
        prev_g = qw_prev_g
        prev_am = qw_prev_am
        for it in range(ntiles):
            b = it % 2
            if prev_v[b]:
                wait("sync", sV, prev_v[b])
            if prev_am[b]:
                wait("sync", sV, prev_am[b])
            vld = dma("sync", wrow[b][:, 0:ncols],
                      w_in[it * 128:(it + 1) * 128, :])
            wait("vector", sD, vld)
            if prev_g[b]:
                wait("gpsimd", sG, prev_g[b])
            mam = op("vector", lambda e, b=b, c=ncols: e.tensor_reduce(
                out=wamax[:, b:b + 1], in_=wrow[b][:, 0:c], axis=AX,
                op=ALU.max, apply_absolute_value=True), inc=True)
            prev_am[b] = mam
            wait("scalar", sV, mam)
            op("scalar", lambda e, b=b: e.activation(
                out=wamax[:, b + 2:b + 3], in_=wamax[:, b:b + 1], func=AF.Ln))
            msx = op("scalar", lambda e, b=b: e.activation(
                out=wamax[:, b + 2:b + 3], in_=wamax[:, b + 2:b + 3],
                func=AF.Exp, bias=ln224_sb[:, 0:1], scale=-1.0), inc=True)
            wait("vector", sS, msx)
            op("vector", lambda e, b=b, c=ncols: e.tensor_scalar(
                out=wq16[b][:, 0:c], in0=wrow[b][:, 0:c],
                scalar1=wamax[:, b + 2:b + 3], scalar2=None, op0=ALU.mult))
            mv = op("vector", lambda e, b=b, t=it, w=wsc_tile: e.tensor_scalar(
                out=w[:, t:t + 1], in0=wamax[:, b:b + 1],
                scalar1=1.0 / Q224, scalar2=None, op0=ALU.mult), inc=True)
            prev_v[b] = mv
            wait("gpsimd", sV, mv)
            prev_g[b] = dma("gpsimd", q_out[it * 128:(it + 1) * 128, :],
                            wq16[b][:, 0:ncols])

    quant_weight(wg_in, qwg_n, wsc_g, NI, D)
    quant_weight(wu_in, qwu_n, wsc_u, NI, D)
    quant_weight(wd_in, qwd_n, wsc_d, NH, I_FF)
    g_wq = sG.n
    mark("vector")
    wprep.close()

    checkpoint("wq")
    phA = ExitStack()
    xT_sb = sbuf("xT_sb", (128, NH, TOK), F32, phA)
    for c in range(NH):
        dma("sync", xT_sb[:, c, :], xT[c * 128:(c + 1) * 128, :])
    v_xT = sD.n

    a1 = ExitStack()
    wv_sb = sbuf("wv_sb", (128, NH, 512), F32, a1)
    kv_ev = sbuf("kv_ev", (128, 6 * 512), F32, a1)
    wait("tensor", sD, v_xT)
    m_mm_prev = 0
    m_ev_prev = 0
    g_store_prev = [0]
    for ft in range(4):
        if m_mm_prev:
            wait("sync", sPE, m_mm_prev)
        vw = 0
        for hc in range(NH):
            vw = dma("sync", wv_sb[:, hc, :],
                     Wv[hc * 128:(hc + 1) * 128, ft * 512:(ft + 1) * 512])
        wait("tensor", sD, vw)
        for tg in range(3):
            tcs = list(range(tg * 6, min(16, tg * 6 + 6)))
            if m_ev_prev:
                wait("tensor", sV, m_ev_prev)
            for hc in range(NH):
                for j, t0 in enumerate(tcs):
                    mm(lambda e, hc=hc, j=j, t0=t0: e.matmul(
                        pbk[j][:, :],
                        xT_sb[:, hc, t0 * 128:(t0 + 1) * 128],
                        wv_sb[:, hc, :],
                        start=(hc == 0), stop=(hc == NH - 1)))
            m_mm_prev = mark("tensor")
            wait("vector", sPE, m_mm_prev)
            wait("vector", sG, g_store_prev[0])
            for j, t0 in enumerate(tcs):
                op("vector", lambda e, j=j: e.tensor_copy(
                    out=kv_ev[:, j * 512:(j + 1) * 512], in_=pbk[j][:, :]))
            m_ev_prev = mark("vector")
            wait("gpsimd", sV, m_ev_prev)
            for j, t0 in enumerate(tcs):
                g_store_prev[0] = dma(
                    "gpsimd", v_scr[t0 * 128:(t0 + 1) * 128,
                                    ft * 512:(ft + 1) * 512],
                    kv_ev[:, j * 512:(j + 1) * 512])
    barrier()
    a1.close()
    if debug:
        dma_out("gpsimd", dbg["d_v"][:, :], v_scr[:, :])

    checkpoint("a2s")
    a2 = ExitStack()
    wh_sb = sbuf("wh_sb", (128, 2, NH, 128), F32, a2)
    kfm_sb = sbuf("kfm_sb", (128, TOK), F32, a2)
    phk_sb = sbuf("phk_sb", (128, TOK), F32, a2)
    zp_sb = sbuf("zp_sb", (128, 512), F32, a2)
    mn_sb = sbuf("mn_sb", (128, 512), F32, a2)
    zi_sb = sbuf("zi_sb", (128, 512), mybir.dt.int32, a2)
    phkT_sb = sbuf("phkT_sb", (128, 256), F32, a2)
    vld_sb = sbuf("vld_sb", (128, 256), F32, a2)

    m_lastmm2 = [0]
    m_pkstore = [0]
    wh_gate = [0, 0]
    m_ev_a2 = [0, 0]
    for hd in range(H):
        hb = hd % 2
        if wh_gate[hb]:
            wait("sync", sPE, wh_gate[hb])
        vw = 0
        for hc in range(NH):
            vw = dma("sync", wh_sb[:, hb, hc, :],
                     Wk[hc * 128:(hc + 1) * 128, hd * 128:(hd + 1) * 128])
        wait("tensor", sD, vw)
        wait("vector", sG, sG.n)
        for tt in range(4):
            if m_ev_a2[tt % 2]:
                wait("tensor", sV, m_ev_a2[tt % 2])
            for hc in range(NH):
                mm(lambda e, hc=hc, tt=tt, hb=hb: e.matmul(
                    pbk[tt % 2][:, :], wh_sb[:, hb, hc, :],
                    xT_sb[:, hc, tt * 512:(tt + 1) * 512],
                    start=(hc == 0), stop=(hc == NH - 1)))
            mpe = mark("tensor")
            wh_gate[hb] = mpe
            wait("vector", sPE, mpe)
            m_ev_a2[tt % 2] = op("vector", lambda e, tt=tt: e.tensor_copy(
                out=kfm_sb[:, tt * 512:(tt + 1) * 512], in_=pbk[tt % 2][:, :]),
                inc=True)
        m_k = m_ev_a2[3 % 2]
        wait("gpsimd", sV, m_k)
        dma("gpsimd", k_scr[hd * 128:(hd + 1) * 128, :], kfm_sb[:, :])
        wait("scalar", sV, m_k)
        m_phi = 0
        for sp in range(4):
            s0 = sp * 512
            msx = op("scalar", lambda e, s0=s0, hd=hd: e.activation(
                out=phk_sb[:, s0:s0 + 512], in_=kfm_sb[:, s0:s0 + 512],
                func=AF.Exp, bias=pb_sb[:, hd:hd + 1],
                scale=ps_sb[:, hd:hd + 1]), inc=True)
            wait("vector", sS, msx)
            op("vector", lambda e, s0=s0, hd=hd: e.tensor_scalar(
                out=zp_sb[:, :], in0=kfm_sb[:, s0:s0 + 512],
                scalar1=ps_sb[:, hd:hd + 1], scalar2=pb1_sb[:, hd:hd + 1],
                op0=ALU.mult, op1=ALU.add))
            op("vector", lambda e: e.tensor_scalar(
                out=mn_sb[:, :], in0=zp_sb[:, :], scalar1=10.0,
                scalar2=None, op0=ALU.min))
            op("vector", lambda e: e.tensor_scalar(
                out=zi_sb[:, :], in0=zp_sb[:, :], scalar1=1.0, scalar2=None,
                op0=ALU.is_gt))
            m_phi = op("vector", lambda e, s0=s0: e.select(
                out=phk_sb[:, s0:s0 + 512], mask=zi_sb[:, :],
                on_true=mn_sb[:, :], on_false=phk_sb[:, s0:s0 + 512]),
                inc=True)
            wait("scalar", sV, m_phi)
        wait("tensor", sV, m_phi)
        for tp in range(8):
            mpe_t = 0
            for cc in range(2):
                t0 = tp * 2 + cc
                mpe_t = op("tensor", lambda e, t0=t0, cc=cc: e.transpose(
                    pbk[2][:, cc * 128:(cc + 1) * 128],
                    phk_sb[:, t0 * 128:(t0 + 1) * 128], id32_sb[:]), inc=True)
            wait("vector", sPE, mpe_t)
            wait("vector", sG, m_pkstore[0])
            mv = op("vector", lambda e: e.tensor_copy(
                out=phkT_sb[:, :], in_=pbk[2][:, 0:256]), inc=True)
            wait("gpsimd", sV, mv)
            m_pkstore[0] = dma(
                "gpsimd",
                pk_scr[tp * 256:(tp + 1) * 256, hd * 128:(hd + 1) * 128]
                .rearrange("(c p) f -> p c f", c=2),
                phkT_sb[:, :].rearrange("p (c f) -> p c f", c=2))
            wait("sync", sPE, m_lastmm2[0])
            vld = dma("sync",
                      vld_sb[:, :].rearrange("p (c f) -> p c f", c=2),
                      v_scr[tp * 256:(tp + 1) * 256, hd * 128:(hd + 1) * 128]
                      .rearrange("(c p) f -> p c f", c=2))
            wait("tensor", sV, mv)
            wait("tensor", sD, vld)
            for cc in range(2):
                mm(lambda e, cc=cc: e.matmul(
                    pbk[3][:, 0:128], phkT_sb[:, cc * 128:(cc + 1) * 128],
                    vld_sb[:, cc * 128:(cc + 1) * 128],
                    start=True, stop=False))
                mpe2 = op("tensor", lambda e, cc=cc: e.matmul(
                    pbk[3][:, 128:129], phkT_sb[:, cc * 128:(cc + 1) * 128],
                    ones_sb[:, 0:1], start=False, stop=True), inc=True)
                m_lastmm2[0] = mpe2
                wait("vector", sPE, mpe2)
                mva = op("vector", lambda e, hd=hd: e.tensor_tensor(
                    out=dS_acc[:, hd * 129:(hd + 1) * 129],
                    in0=dS_acc[:, hd * 129:(hd + 1) * 129],
                    in1=pbk[3][:, 0:129], op=ALU.add), inc=True)
                wait("tensor", sV, mva)
    m_A = mark("vector")
    if debug:
        wait("gpsimd", sV, m_A)
        dma_out("gpsimd", dbg["d_dS"][:, :], dS_acc[:, :])
        dma_out("gpsimd", dbg["d_k"][:, :], k_scr[:, :])
        dma_out("gpsimd", dbg["d_pk"][:, :], pk_scr[:, :])
    barrier()
    a2.close()

    checkpoint("b0s")
    b0 = ExitStack()
    wqh_sb = sbuf("wqh_sb", (128, 2, NH, 128), F32, b0)
    qfm_sb = sbuf("qfm_sb", (128, TOK), F32, b0)
    wq_gate = [0, 0]
    m_ev_b0 = [0, 0]
    for hd in range(H):
        hb = hd % 2
        if wq_gate[hb]:
            wait("sync", sPE, wq_gate[hb])
        vw = 0
        for hc in range(NH):
            vw = dma("sync", wqh_sb[:, hb, hc, :],
                     Wq[hc * 128:(hc + 1) * 128, hd * 128:(hd + 1) * 128])
        wait("tensor", sD, vw)
        wait("vector", sG, sG.n)
        for tt in range(4):
            if m_ev_b0[tt % 2]:
                wait("tensor", sV, m_ev_b0[tt % 2])
            for hc in range(NH):
                mm(lambda e, hc=hc, tt=tt, hb=hb: e.matmul(
                    pbk[4 + tt % 2][:, :], wqh_sb[:, hb, hc, :],
                    xT_sb[:, hc, tt * 512:(tt + 1) * 512],
                    start=(hc == 0), stop=(hc == NH - 1)))
            mpe = mark("tensor")
            wq_gate[hb] = mpe
            wait("vector", sPE, mpe)
            m_ev_b0[tt % 2] = op("vector", lambda e, tt=tt: e.tensor_copy(
                out=qfm_sb[:, tt * 512:(tt + 1) * 512],
                in_=pbk[4 + tt % 2][:, :]), inc=True)
        wait("gpsimd", sV, max(m_ev_b0))
        dma("gpsimd", q_scr[hd * 128:(hd + 1) * 128, :], qfm_sb[:, :])
    if debug:
        dma_out("gpsimd", dbg["d_q"][:, :], q_scr[:, :])
    barrier()
    b0.close()
    phA.close()

    checkpoint("b0")
    exch = ExitStack()
    gth = sbuf("gth", (128, H * 129), F32, exch)
    dma("gpsimd", cc_in[:, :], dS_acc[:, :])
    v_ccin = sG.n
    streams["gpsimd"].append(lambda e, v=v_ccin: e.wait_ge(sG.sem, v))
    sCC.n += 1
    streams["gpsimd"].append(lambda e: e.collective_compute(
        "AllGather", ALU.bypass, ins=[cc_in[:, :]], outs=[cc_out[:, :]],
        replica_groups=[list(range(N_CORES))]).then_inc(sCC.sem, 1))
    wait("sync", sCC, sCC.n)
    for r in range(N_CORES):
        vld = dma("sync", gth[:, :], cc_out[r * 128:(r + 1) * 128, :])
        wait("vector", sD, vld)
        m_acc = op("vector", lambda e, r=r: e.scalar_tensor_tensor(
            out=S_sb[:], in0=gth[:, :], scalar=mask_sb[:, r:r + 1],
            in1=S_sb[:], op0=ALU.mult, op1=ALU.add), inc=True)
        if r + 1 < N_CORES:
            wait("sync", sV, m_acc)
    barrier()
    exch.close()

    wtr = ExitStack()
    ttile = [sbuf(f"ttile{i}", (128, I_FF), BF16, wtr) for i in range(2)]
    t8 = [sbuf(f"t8_{i}", (128, I_FF), FP8, wtr) for i in range(2)]

    tw_prev_v = [0, 0]
    tw_prev_g = [0, 0]

    def transpose_weight(nat, trs, nchunks, ncols):
        prev_v = tw_prev_v
        prev_g = tw_prev_g
        for ch in range(nchunks):
            b = ch % 2
            if prev_v[b]:
                wait("sync", sV, prev_v[b])
            vld = dma("sync", ttile[b][:, 0:ncols],
                      nat[0:ncols, ch * 128:(ch + 1) * 128], transpose=True)
            wait("vector", sD, vld)
            if prev_g[b]:
                wait("vector", sG, prev_g[b])
            mv = op("vector", lambda e, b=b, c=ncols: e.tensor_copy(
                out=t8[b][:, 0:c], in_=ttile[b][:, 0:c]), inc=True)
            prev_v[b] = mv
            wait("gpsimd", sV, mv)
            prev_g[b] = dma("gpsimd", trs[ch * 128:(ch + 1) * 128, :],
                            t8[b][:, 0:ncols])

    wait("sync", sG, g_wq)
    transpose_weight(qwg_n, qwg_t, NH, I_FF)
    transpose_weight(qwu_n, qwu_t, NH, I_FF)
    transpose_weight(qwd_n, qwd_t, NI, D)
    g_wt = sG.n
    wtr.close()

    checkpoint("wt")

    checkpoint("exch")
    ba = ExitStack()
    q_sb = sbuf("q_sb", (128, H, TA), F32, ba)
    k_sb = sbuf("k_sb", (128, H, TA), F32, ba)
    pq_sb = sbuf("pq_sb", (128, H, TA), F32, ba)
    zpb_sb = sbuf("zpb_sb", (128, TA), F32, ba)
    mnb_sb = sbuf("mnb_sb", (128, TA), F32, ba)
    zbi_sb = sbuf("zbi_sb", (128, TA), mybir.dt.int32, ba)
    pkT_sb = sbuf("pkT_sb", (64, 2, H * 128), F32, ba)
    va_sb = sbuf("va_sb", (64, 2, H, 129), F32, ba)
    E_sb = sbuf("E_sb", (64, H * 64), F32, ba)
    rcp_sb = sbuf("rcp_sb", (64, 3 * H), F32, ba)
    hyb_sb = sbuf("hyb_sb", (64, H * 128), F32, ba)
    hfm_sb = sbuf("hfm_sb", (128, H, TA), F32, ba)
    xn_sb = sbuf("xn_sb", (128, NCH, D), F32, ba)
    wo_sb = sbuf("wo_sb", (128, NH, 256), F32, ba)
    x1_sb = sbuf("x1_sb", (128, NCH, D), F32, ba)
    sq_sb = sbuf("sq_sb", (128, D), F32, ba)
    snap_sb = sbuf("snap_sb", (128, 16), F32, ba)

    op("vector", lambda e: e.memset(va_sb[:, :, :, 128:129], 1.0))
    mark("vector")

    for tt in range(NT):
        tb = tt * TA
        vq = vk = 0
        for hd in range(H):
            vq = dma("sync", q_sb[:, hd, :],
                     q_scr[hd * 128:(hd + 1) * 128, tb:tb + TA])
            vk = dma("sync", k_sb[:, hd, :],
                     k_scr[hd * 128:(hd + 1) * 128, tb:tb + TA])
        vxn = 0
        for cch in range(NCH):
            vxn = dma("sync", xn_sb[:, cch, :],
                      xN[tb + cch * 128:tb + (cch + 1) * 128, :])
        wait("scalar", sD, vq)
        m_pq = 0
        for hd in range(H):
            msx = op("scalar", lambda e, hd=hd: e.activation(
                out=pq_sb[:, hd, :], in_=q_sb[:, hd, :], func=AF.Exp,
                bias=pb_sb[:, hd:hd + 1], scale=ps_sb[:, hd:hd + 1]), inc=True)
            wait("vector", sS, msx)
            op("vector", lambda e, hd=hd: e.tensor_scalar(
                out=zpb_sb[:, :], in0=q_sb[:, hd, :],
                scalar1=ps_sb[:, hd:hd + 1], scalar2=pb1_sb[:, hd:hd + 1],
                op0=ALU.mult, op1=ALU.add))
            op("vector", lambda e: e.tensor_scalar(
                out=mnb_sb[:, :], in0=zpb_sb[:, :], scalar1=10.0,
                scalar2=None, op0=ALU.min))
            op("vector", lambda e: e.tensor_scalar(
                out=zbi_sb[:, :], in0=zpb_sb[:, :], scalar1=1.0,
                scalar2=None, op0=ALU.is_gt))
            m_pq = op("vector", lambda e, hd=hd: e.select(
                out=pq_sb[:, hd, :], mask=zbi_sb[:, :],
                on_true=mnb_sb[:, :], on_false=pq_sb[:, hd, :]), inc=True)
            wait("scalar", sV, m_pq)

        for bp in range(2):
            if tt > 0 or bp > 0:
                wait("sync", sPE, sPE.n)
            vv = vp = 0
            for blk2 in range(2):
                r0 = tb + bp * 128 + blk2 * 64
                vv = dma("sync", va_sb[:, blk2, :, 0:128],
                         v_scr[r0:r0 + 64, :])
                vp = dma("sync", pkT_sb[:, blk2, :], pk_scr[r0:r0 + 64, :])
            for blk2 in range(2):
                b = bp * 2 + blk2
                ts = b * 64
                wait("tensor", sD, max(vq, vk, vv, vp))
                wait("tensor", sV, m_pq)
                for g in range(2):
                    for h8 in range(8):
                        hd = g * 8 + h8
                        mm(lambda e, hd=hd, h8=h8, ts=ts: e.matmul(
                            pbk[0][0:64, h8 * 64:(h8 + 1) * 64],
                            k_sb[:, hd, ts:ts + 64],
                            q_sb[:, hd, ts:ts + 64],
                            start=True, stop=True))
                    mpe = mark("tensor")
                    wait("scalar", sPE, mpe)
                    mE = op("scalar", lambda e, g=g: e.activation(
                        out=E_sb[:, g * 512:(g + 1) * 512],
                        in_=pbk[0][0:64, :], func=AF.Exp,
                        bias=cC_sb[0:64, 0:1], scale=SCALE), inc=True)
                    wait("tensor", sS, mE)
                for hd in range(H):
                    mm(lambda e, hd=hd: e.matmul(
                        pbk[1][0:64, hd:hd + 1],
                        E_sb[:, hd * 64:(hd + 1) * 64],
                        ones_sb[0:64, 0:1],
                        start=(hd == 0), stop=False))
                for hd in range(H):
                    bank = 4 + hd // 4
                    off = (hd % 4) * 128
                    mm(lambda e, hd=hd, bank=bank, off=off, blk2=blk2:
                       e.matmul(
                           pbk[bank][0:64, off:off + 128],
                           E_sb[:, hd * 64:(hd + 1) * 64],
                           va_sb[:, blk2, hd, 0:128], start=True, stop=True))
                mpe_loc = mark("tensor")
                wait("vector", sPE, mpe_loc)
                if debug and tt == 0 and bp == 0 and blk2 == 0:
                    mvd = op("vector", lambda e: e.tensor_copy(
                        out=mnb_sb[0:64, 0:128], in_=pbk[4][0:64, 0:128]),
                        inc=True)
                    wait("gpsimd", sV, mvd)
                    dma_out("gpsimd", dbg["d_lraw"][:, :], mnb_sb[0:64, 0:128])
                wait("scalar", sPE, mpe_loc)
                op("scalar", lambda e: e.activation(
                    out=rcp_sb[:, 2 * H:3 * H], in_=pbk[1][0:64, 0:H],
                    func=AF.Ln))
                msm = op("scalar", lambda e: e.activation(
                    out=rcp_sb[:, 0:H], in_=rcp_sb[:, 2 * H:3 * H],
                    func=AF.Exp, scale=-1.0), inc=True)
                wait("vector", sS, msm)
                for hd in range(H):
                    bank = 4 + hd // 4
                    off = (hd % 4) * 128
                    op("vector", lambda e, hd=hd, bank=bank, off=off:
                       e.tensor_scalar(
                           out=hyb_sb[:, hd * 128:(hd + 1) * 128],
                           in0=pbk[bank][0:64, off:off + 128],
                           scalar1=rcp_sb[:, hd:hd + 1], scalar2=None,
                           op0=ALU.mult))
                m_loc = mark("vector")
                if debug and tt == 0 and bp == 0 and blk2 == 0:
                    wait("gpsimd", sV, m_loc)
                    dma_out("gpsimd", dbg["d_loc"][:, :], hyb_sb[:, :])
                wait("tensor", sV, m_loc)
                for hd in range(H):
                    bk = 2 + hd % 2
                    mm(lambda e, hd=hd, bk=bk, blk2=blk2: e.matmul(
                        pbk[bk][:, 0:128],
                        pkT_sb[:, blk2, hd * 128:(hd + 1) * 128],
                        va_sb[:, blk2, hd, 0:128], start=True, stop=False))
                    mpe = op("tensor", lambda e, hd=hd, bk=bk, blk2=blk2:
                             e.matmul(
                                 pbk[bk][:, 128:129],
                                 pkT_sb[:, blk2, hd * 128:(hd + 1) * 128],
                                 ones_sb[0:64, 0:1], start=False, stop=True),
                             inc=True)
                    wait("vector", sPE, mpe)
                    m_Sup = op("vector", lambda e, hd=hd, bk=bk:
                               e.tensor_tensor(
                                   out=S_sb[:, hd * 129:(hd + 1) * 129],
                                   in0=S_sb[:, hd * 129:(hd + 1) * 129],
                                   in1=pbk[bk][:, 0:129], op=ALU.add),
                               inc=True)
                    wait("tensor", sV, m_Sup)
                    nbank = 4 + hd // 4
                    noff = (hd % 4) * 128
                    mm(lambda e, hd=hd, nbank=nbank, noff=noff, ts=ts:
                       e.matmul(
                           pbk[nbank][0:64, noff:noff + 128],
                           pq_sb[:, hd, ts:ts + 64],
                           S_sb[:, hd * 129:hd * 129 + 128],
                           start=True, stop=True))
                    mm(lambda e, hd=hd, ts=ts: e.matmul(
                        pbk[1][0:64, H + hd:H + hd + 1],
                        pq_sb[:, hd, ts:ts + 64],
                        S_sb[:, hd * 129 + 128:hd * 129 + 129],
                        start=False, stop=(hd == H - 1)))
                mpe_num = mark("tensor")
                wait("vector", sPE, mpe_num)
                if debug and tt == 0 and bp == 0 and blk2 == 0:
                    mvd = op("vector", lambda e: e.tensor_copy(
                        out=mnb_sb[0:64, 0:32], in_=pbk[1][0:64, 0:32]),
                        inc=True)
                    wait("gpsimd", sV, mvd)
                    dma_out("gpsimd", dbg["d_den"][:, :], mnb_sb[0:64, 0:32])
                mgl = op("vector", lambda e: e.tensor_scalar(
                    out=rcp_sb[:, 2 * H:3 * H], in0=pbk[1][0:64, H:2 * H],
                    scalar1=EPS_DEN, scalar2=None, op0=ALU.max), inc=True)
                wait("scalar", sV, mgl)
                op("scalar", lambda e: e.activation(
                    out=rcp_sb[:, 2 * H:3 * H], in_=rcp_sb[:, 2 * H:3 * H],
                    func=AF.Ln))
                mgl2 = op("scalar", lambda e: e.activation(
                    out=rcp_sb[:, H:2 * H], in_=rcp_sb[:, 2 * H:3 * H],
                    func=AF.Exp, scale=-1.0), inc=True)
                wait("vector", sS, mgl2)
                for hd in range(H):
                    nbank = 4 + hd // 4
                    noff = (hd % 4) * 128
                    op("vector", lambda e, hd=hd, nbank=nbank, noff=noff:
                       e.scalar_tensor_tensor(
                           out=hyb_sb[:, hd * 128:(hd + 1) * 128],
                           in0=pbk[nbank][0:64, noff:noff + 128],
                           scalar=rcp_sb[:, H + hd:H + hd + 1],
                           in1=hyb_sb[:, hd * 128:(hd + 1) * 128],
                           op0=ALU.mult, op1=ALU.add))
                m_hyb = mark("vector")
                if debug and tt == 0 and bp == 0 and blk2 == 0:
                    wait("gpsimd", sV, m_hyb)
                    dma_out("gpsimd", dbg["d_S"][:, :], S_sb[:, :])
                    dma_out("gpsimd", dbg["d_E"][:, :], E_sb[:, :])
                    dma_out("gpsimd", dbg["d_rcp"][:, :], rcp_sb[:, 16:48])
                    dma_out("gpsimd", dbg["d_hblk"][:, :], hyb_sb[:, :])
                wait("tensor", sV, m_hyb)
                for grp in range(2):
                    bk = 2 + grp
                    for j in range(8):
                        hd = grp * 8 + j
                        mm(lambda e, hd=hd, j=j, bk=bk: e.transpose(
                            pbk[bk][:, j * 64:(j + 1) * 64],
                            hyb_sb[:, hd * 128:(hd + 1) * 128],
                            id32_sb[0:64, 0:64]))
                    mpe = mark("tensor")
                    wait("vector", sPE, mpe)
                    mv = op("vector", lambda e, grp=grp, ts=ts, bk=bk:
                            e.tensor_copy(
                                out=hfm_sb[:, grp * 8:(grp + 1) * 8,
                                           ts:ts + 64],
                                in_=pbk[bk][:, 0:512]
                                .rearrange("p (a b) -> p a b", a=8)),
                            inc=True)
                    wait("tensor", sV, mv)
        m_hfm = mark("vector")
        if debug:
            wait("gpsimd", sV, m_hfm)
            for hd in range(H):
                dma_out("gpsimd",
                        dbg["d_hyb"][hd * 128:(hd + 1) * 128, tb:tb + TA],
                        hfm_sb[:, hd, :])
        wait("tensor", sV, m_hfm)
        for htile in range(4):
            for half in range(2):
                c0 = htile * 512 + half * 256
                if htile > 0 or half > 0 or tt > 0:
                    wait("scalar", sPE, sPE.n)
                vwo = 0
                for hc in range(NH):
                    vwo = dma("scalar", wo_sb[:, hc, :],
                              Wo[hc * 128:(hc + 1) * 128, c0:c0 + 256])
                wait("tensor", sA, vwo)
                if htile == 0 and half == 0:
                    wait("tensor", sD, vxn)
                for cch in range(NCH):
                    for hc in range(NH):
                        mm(lambda e, hc=hc, cch=cch: e.matmul(
                            pbk[cch][:, 0:256],
                            hfm_sb[:, hc, cch * 128:(cch + 1) * 128],
                            wo_sb[:, hc, :], start=(hc == 0), stop=False))
                    mm(lambda e, cch=cch, c0=c0: e.matmul(
                        pbk[cch][:, 0:256], ones_sb[0:1, 0:128],
                        bo_sb[0:1, c0:c0 + 256], start=False, stop=True))
                mpe = mark("tensor")
                wait("vector", sPE, mpe)
                wait("vector", sD, vxn)
                for cch in range(NCH):
                    op("vector", lambda e, cch=cch, c0=c0: e.tensor_tensor(
                        out=x1_sb[:, cch, c0:c0 + 256],
                        in0=pbk[cch][:, 0:256],
                        in1=xn_sb[:, cch, c0:c0 + 256], op=ALU.add))
                m_ox = mark("vector")
                wait("tensor", sV, m_ox)
        if debug and tt == 0:
            wait("gpsimd", sV, m_ox)
            for cch in range(NCH):
                dma_out("gpsimd", dbg["d_pre"][cch * 128:(cch + 1) * 128, :],
                        x1_sb[:, cch, :])
            wait("vector", sO, sO.n)
            wait("scalar", sO, sO.n)
        wait("scalar", sV, m_ox)
        for cch in range(NCH):
            ms = op("scalar", lambda e, cch=cch: e.activation(
                out=sq_sb[:, :], in_=x1_sb[:, cch, :], func=AF.Square,
                accum_out=stat[:, 0:1]), inc=True)
            op("vector", lambda e, cch=cch: e.tensor_reduce(
                out=stat[:, 12:13], in_=x1_sb[:, cch, :], axis=AX, op=ALU.add))
            op("vector", lambda e: e.drain())
            wait("vector", sS, ms)
            op("vector", lambda e: e.tensor_scalar(
                out=stat[:, 12:13], in0=stat[:, 12:13], scalar1=1.0 / D,
                scalar2=None, op0=ALU.mult))
            op("vector", lambda e: e.tensor_scalar(
                out=stat[:, 0:1], in0=stat[:, 0:1], scalar1=1.0 / D,
                scalar2=None, op0=ALU.mult))
            op("vector", lambda e: e.drain())
            op("vector", lambda e: e.tensor_tensor(
                out=stat[:, 2:3], in0=stat[:, 12:13], in1=stat[:, 12:13],
                op=ALU.mult))
            op("vector", lambda e: e.drain())
            mv = op("vector", lambda e: e.tensor_tensor(
                out=stat[:, 2:3], in0=stat[:, 0:1], in1=stat[:, 2:3],
                op=ALU.subtract), inc=True)
            wait("scalar", sV, mv)
            op("scalar", lambda e: e.activation(
                out=stat[:, 3:4], in_=stat[:, 2:3], func=AF.Ln,
                bias=ceps_sb[:, 0:1], scale=1.0))
            ms2 = op("scalar", lambda e: e.activation(
                out=stat[:, 4:5], in_=stat[:, 3:4], func=AF.Exp,
                scale=-0.5), inc=True)
            wait("vector", sS, ms2)
            if debug and tt == 0 and cch == 1:
                op("vector", lambda e: e.tensor_copy(
                    out=snap_sb[:, 0:8], in_=x1_sb[:, 1, 0:8]))
                msnap = op("vector", lambda e: e.tensor_copy(
                    out=snap_sb[:, 8:16], in_=stat[:, 8:16]), inc=True)
                wait("gpsimd", sV, msnap)
                dma_out("gpsimd", dbg["d_stat"][:, :], snap_sb[:, :])
            m_x1 = op("vector", lambda e, cch=cch: e.tensor_scalar(
                out=x1_sb[:, cch, :], in0=x1_sb[:, cch, :],
                scalar1=stat[:, 12:13], scalar2=stat[:, 4:5],
                op0=ALU.subtract, op1=ALU.mult), inc=True)
            wait("gpsimd", sV, m_x1)
            dma("gpsimd", x1_scr[tb + cch * 128:tb + (cch + 1) * 128, :],
                x1_sb[:, cch, :])
            if debug:
                dma_out("gpsimd",
                        dbg["d_x1"][tb + cch * 128:tb + (cch + 1) * 128, :],
                        x1_sb[:, cch, :])
        barrier()
    ba.close()

    checkpoint("battn")
    bf = ExitStack()
    x1f_sb = sbuf("x1f_sb", (128, NCH, D), F32, bf)
    qxT_sb = sbuf("qxT_sb", (128, NH, NCH * 128), FP8, bf)
    g_sb = sbuf("g_sb", (128, NCH, I_FF), F32, bf)
    qtT_sb = sbuf("qtT_sb", (128, NI, NCH * 128), FP8, bf)
    diag_sb = sbuf("diag_sb", (128, 128), F32, bf)
    wstr_sb = sbuf("wstr_sb", (128, 2, 1536), FP8, bf)
    wdstr_sb = sbuf("wdstr_sb", (128, 2, D), FP8, bf)
    utmp_sb = sbuf("utmp_sb", (128, 512), F32, bf)
    y_sb = sbuf("y_sb", (128, NCH, D), F32, bf)
    wsgb_sb = sbuf("wsgb_sb", (128, I_FF), F32, bf)
    wsub_sb = sbuf("wsub_sb", (128, I_FF), F32, bf)
    wsdb_sb = sbuf("wsdb_sb", (128, D), F32, bf)
    trs_sb = sbuf("trs_sb", (128, 128), F32, bf)
    sx_sb = sbuf("sx_sb", (128, 2 * NCH), F32, bf)

    def bcast_scales(wsc_tile, ntiles, bc_out):
        mpe = op("tensor", lambda e, n=ntiles: e.transpose(
            pbk[7][0:n, 0:128], wsc_tile[:, 0:n], id32_sb[:]), inc=True)
        wait("vector", sPE, mpe)
        mv = op("vector", lambda e, n=ntiles: e.tensor_copy(
            out=trs_sb[0:n, :], in_=pbk[7][0:n, 0:128]), inc=True)
        wait("sync", sV, mv)
        dma("sync",
            bc_out[0:1, 0:ntiles * 128].rearrange("p (a b) -> p a b", a=ntiles),
            trs_sb[0:ntiles, :])
        wait("tensor", sD, sD.n)
        wait("tensor", sV, mv)
        nblk = (ntiles * 128 + 511) // 512
        for cb in range(nblk):
            c0 = cb * 512
            cw = min(512, ntiles * 128 - c0)
            mpe = op("tensor", lambda e, c0=c0, cw=cw: e.matmul(
                pbk[7][:, 0:cw], ones_sb[0:1, 0:128],
                bc_out[0:1, c0:c0 + cw], start=True, stop=True), inc=True)
            wait("vector", sPE, mpe)
            mv = op("vector", lambda e, c0=c0, cw=cw, b=bc_out: e.tensor_copy(
                out=b[:, c0:c0 + cw], in_=pbk[7][:, 0:cw]), inc=True)
            wait("tensor", sV, mv)
        mp = mark("vector")
        wait("gpsimd", sV, mp)
        wait("scalar", sV, mp)
        return mp

    m_tmp = bcast_scales(wsc_g, NI, wsgb_sb)
    if debug:
        wait("gpsimd", sV, m_tmp)
        dma_out("gpsimd", dbg["d_wsg"][:, :], wsgb_sb[:, :])
        wait("vector", sO, sO.n)
    bcast_scales(wsc_u, NI, wsub_sb)
    m_bc = bcast_scales(wsc_d, NH, wsdb_sb)
    wait("sync", sV, m_bc)
    wait("sync", sG, g_wt)
    wait("scalar", sG, g_wt)

    FF_G = [(0, 3), (3, 3), (6, 3), (9, 2)]

    for tt in range(NT):
        tb = tt * TA
        vx1 = 0
        for cch in range(NCH):
            vx1 = dma("sync", x1f_sb[:, cch, :],
                      x1_scr[tb + cch * 128:tb + (cch + 1) * 128, :])
        wait("vector", sD, vx1)
        for cch in range(NCH):
            mam = op("vector", lambda e, cch=cch: e.tensor_reduce(
                out=stat[:, 8:9], in_=x1f_sb[:, cch, :], axis=AX,
                op=ALU.max, apply_absolute_value=True), inc=True)
            op("vector", lambda e: e.drain())
            wait("scalar", sV, mam)
            op("scalar", lambda e: e.activation(
                out=stat[:, 9:10], in_=stat[:, 8:9], func=AF.Ln))
            msx = op("scalar", lambda e: e.activation(
                out=stat[:, 9:10], in_=stat[:, 9:10], func=AF.Exp,
                bias=ln224_sb[:, 0:1], scale=-1.0), inc=True)
            op("vector", lambda e, cch=cch: e.tensor_scalar(
                out=sx_sb[:, cch:cch + 1], in0=stat[:, 8:9],
                scalar1=1.0 / Q224, scalar2=None, op0=ALU.mult))
            wait("vector", sS, msx)
            m_dg = op("vector", lambda e: e.tensor_scalar(
                out=diag_sb[:, cch * 0:128] if False else diag_sb[:, :],
                in0=id32_sb[:, :],
                scalar1=stat[:, 9:10], scalar2=None, op0=ALU.mult), inc=True)
            wait("tensor", sV, m_dg)
            ev_gate = [0, 0]
            for q4 in range(NH // 4):
                bk = 6 + q4 % 2
                if ev_gate[q4 % 2]:
                    wait("tensor", sV, ev_gate[q4 % 2])
                mpe = 0
                for j in range(4):
                    hc = q4 * 4 + j
                    mpe = op("tensor", lambda e, cch=cch, hc=hc, j=j, bk=bk:
                             e.matmul(
                                 pbk[bk][:, j * 128:(j + 1) * 128],
                                 x1f_sb[:, cch, hc * 128:(hc + 1) * 128],
                                 diag_sb[:, :], start=True, stop=True),
                             inc=True)
                wait("vector", sPE, mpe)
                ev_gate[q4 % 2] = op(
                    "vector", lambda e, cch=cch, q4=q4, bk=bk: e.tensor_copy(
                        out=qxT_sb[:, q4 * 4:(q4 + 1) * 4,
                                   cch * 128:(cch + 1) * 128]
                        .rearrange("p a b -> p (a b)") if False else
                        qxT_sb[:, q4 * 4, cch * 128:cch * 128 + 128]
                        .tensor[...] if False else
                        qxT_sb[:, q4 * 4:q4 * 4 + 4, cch * 128:(cch + 1) * 128],
                        in_=pbk[bk][:, 0:512].rearrange("p (a b) -> p a b", a=4)),
                    inc=True)
        m_qxT = mark("vector")
        if debug and tt == 0:
            for hc in range(NH):
                mqd = op("vector", lambda e, hc=hc: e.tensor_copy(
                    out=y_sb[:, 0, 0:256], in_=qxT_sb[:, hc, :]), inc=True)
                wait("gpsimd", sV, mqd)
                dma_out("gpsimd", dbg["d_qx"][:, hc * 256:(hc + 1) * 256],
                        y_sb[:, 0, 0:256])
                wait("vector", sO, sO.n)
        wait("tensor", sV, m_qxT)

        pe_h = [0, 0]
        for wpass in range(2):
            w_t = qwg_t if wpass == 0 else qwu_t
            wsb = wsgb_sb if wpass == 0 else wsub_sb
            for (f0t, nft) in FF_G:
                f0 = f0t * 512
                nf = nft * 512
                for hc in range(NH):
                    bsl = hc % 2
                    if pe_h[bsl]:
                        wait("scalar", sPE, pe_h[bsl])
                    vw = dma("scalar", wstr_sb[:, bsl, 0:nf],
                             w_t[hc * 128:(hc + 1) * 128, f0:f0 + nf])
                    wait("tensor", sA, vw)

                    for cch in range(NCH):
                        for fl in range(nft):
                            mm(lambda e, hc=hc, cch=cch, fl=fl, bsl=bsl:
                               e.matmul(
                                   pbk[cch * 3 + fl][:, :],
                                   qxT_sb[:, hc, cch * 128:(cch + 1) * 128],
                                   wstr_sb[:, bsl, fl * 512:(fl + 1) * 512],
                                   start=(hc == 0), stop=(hc == NH - 1)))
                    pe_h[bsl] = mark("tensor")
                mpe_gi = pe_h[(NH - 1) % 2]
                wait("vector", sPE, mpe_gi)
                if debug and tt == 0 and wpass == 0 and f0 == 0:
                    mqd = op("vector", lambda e: e.tensor_copy(
                        out=utmp_sb[:, :], in_=pbk[0][:, :]), inc=True)
                    wait("gpsimd", sV, mqd)
                    dma_out("gpsimd", dbg["d_qw"][:, 0:512], utmp_sb[:, :])
                    wait("vector", sO, sO.n)
                for cch in range(NCH):
                    for fl in range(nft):
                        fo = f0 + fl * 512
                        if wpass == 0:
                            op("vector", lambda e, cch=cch, fl=fl, fo=fo,
                               wsb=wsb: e.scalar_tensor_tensor(
                                   out=g_sb[:, cch, fo:fo + 512],
                                   in0=pbk[cch * 3 + fl][:, :],
                                   scalar=sx_sb[:, cch:cch + 1],
                                   in1=wsb[:, fo:fo + 512],
                                   op0=ALU.mult, op1=ALU.mult))
                        else:
                            op("vector", lambda e, cch=cch, fl=fl, fo=fo,
                               wsb=wsb: e.scalar_tensor_tensor(
                                   out=utmp_sb[:, :],
                                   in0=pbk[cch * 3 + fl][:, :],
                                   scalar=sx_sb[:, cch:cch + 1],
                                   in1=wsb[:, fo:fo + 512],
                                   op0=ALU.mult, op1=ALU.mult))
                            op("vector", lambda e, cch=cch, fo=fo:
                               e.tensor_tensor(
                                   out=g_sb[:, cch, fo:fo + 512],
                                   in0=g_sb[:, cch, fo:fo + 512],
                                   in1=utmp_sb[:, :], op=ALU.mult))
                m_ev = mark("vector")
                wait("tensor", sV, m_ev)
            if wpass == 0:
                m_g = mark("vector")
                if debug and tt == 0:
                    wait("gpsimd", sV, m_g)
                    for cch in range(NCH):
                        dma_out("gpsimd",
                                dbg["d_g"][cch * 128:(cch + 1) * 128, :],
                                g_sb[:, cch, :])
                    wait("scalar", sO, sO.n)
                wait("scalar", sV, m_g)
                msl = 0
                for cch in range(NCH):
                    msl = op("scalar", lambda e, cch=cch: e.activation(
                        out=g_sb[:, cch, :], in_=g_sb[:, cch, :],
                        func=AF.Silu), inc=True)
                wait("vector", sS, msl)
        if debug:
            m_t = mark("vector")
            wait("gpsimd", sV, m_t)
            for cch in range(NCH):
                dma_out("gpsimd",
                        dbg["d_t"][tb + cch * 128:tb + (cch + 1) * 128, :],
                        g_sb[:, cch, :])
        for cch in range(NCH):
            mam = op("vector", lambda e, cch=cch: e.tensor_reduce(
                out=stat[:, 10:11], in_=g_sb[:, cch, :], axis=AX,
                op=ALU.max, apply_absolute_value=True), inc=True)
            op("vector", lambda e: e.drain())
            wait("scalar", sV, mam)
            op("scalar", lambda e: e.activation(
                out=stat[:, 11:12], in_=stat[:, 10:11], func=AF.Ln))
            msx = op("scalar", lambda e: e.activation(
                out=stat[:, 11:12], in_=stat[:, 11:12], func=AF.Exp,
                bias=ln224_sb[:, 0:1], scale=-1.0), inc=True)
            op("vector", lambda e, cch=cch: e.tensor_scalar(
                out=sx_sb[:, NCH + cch:NCH + cch + 1], in0=stat[:, 10:11],
                scalar1=1.0 / Q224, scalar2=None, op0=ALU.mult))
            wait("vector", sS, msx)
            m_dg = op("vector", lambda e: e.tensor_scalar(
                out=diag_sb[:, :], in0=id32_sb[:, :],
                scalar1=stat[:, 11:12], scalar2=None, op0=ALU.mult), inc=True)
            wait("tensor", sV, m_dg)
            ev_gate = [0, 0]
            for q4 in range(NI // 4):
                bk = 6 + q4 % 2
                if ev_gate[q4 % 2]:
                    wait("tensor", sV, ev_gate[q4 % 2])
                mpe = 0
                for j in range(4):
                    ic = q4 * 4 + j
                    mpe = op("tensor", lambda e, cch=cch, ic=ic, j=j, bk=bk:
                             e.matmul(
                                 pbk[bk][:, j * 128:(j + 1) * 128],
                                 g_sb[:, cch, ic * 128:(ic + 1) * 128],
                                 diag_sb[:, :], start=True, stop=True),
                             inc=True)
                wait("vector", sPE, mpe)
                ev_gate[q4 % 2] = op(
                    "vector", lambda e, cch=cch, q4=q4, bk=bk: e.tensor_copy(
                        out=qtT_sb[:, q4 * 4:q4 * 4 + 4,
                                   cch * 128:(cch + 1) * 128],
                        in_=pbk[bk][:, 0:512].rearrange("p (a b) -> p a b", a=4)),
                    inc=True)
        m_qtT = mark("vector")
        wait("tensor", sV, m_qtT)
        pe_d = [0, 0]
        for ic in range(NI):
            bsl = ic % 2
            if pe_d[bsl]:
                wait("scalar", sPE, pe_d[bsl])
            vw = dma("scalar", wdstr_sb[:, bsl, :],
                     qwd_t[ic * 128:(ic + 1) * 128, :])
            wait("tensor", sA, vw)
            for cch in range(NCH):
                for ht in range(4):
                    mm(lambda e, ic=ic, cch=cch, ht=ht, bsl=bsl: e.matmul(
                        pbk[cch * 4 + ht][:, :],
                        qtT_sb[:, ic, cch * 128:(cch + 1) * 128],
                        wdstr_sb[:, bsl, ht * 512:(ht + 1) * 512],
                        start=(ic == 0), stop=(ic == NI - 1)))
            pe_d[bsl] = mark("tensor")
        mpe_d = pe_d[(NI - 1) % 2]
        wait("vector", sPE, mpe_d)
        for cch in range(NCH):
            for ht in range(4):
                op("vector", lambda e, cch=cch, ht=ht: e.scalar_tensor_tensor(
                    out=y_sb[:, cch, ht * 512:(ht + 1) * 512],
                    in0=pbk[cch * 4 + ht][:, :],
                    scalar=sx_sb[:, NCH + cch:NCH + cch + 1],
                    in1=wsdb_sb[:, ht * 512:(ht + 1) * 512],
                    op0=ALU.mult, op1=ALU.mult))
            op("vector", lambda e, cch=cch: e.tensor_tensor(
                out=y_sb[:, cch, :], in0=y_sb[:, cch, :],
                in1=x1f_sb[:, cch, :], op=ALU.add))
        m_y = mark("vector")
        wait("tensor", sV, m_y)
        wait("scalar", sV, m_y)
        for cch in range(NCH):
            ms = op("scalar", lambda e, cch=cch: e.activation(
                out=x1f_sb[:, 0, :], in_=y_sb[:, cch, :], func=AF.Square,
                accum_out=stat[:, 0:1]), inc=True)
            op("vector", lambda e, cch=cch: e.tensor_reduce(
                out=stat[:, 12:13], in_=y_sb[:, cch, :], axis=AX, op=ALU.add))
            op("vector", lambda e: e.drain())
            wait("vector", sS, ms)
            op("vector", lambda e: e.tensor_scalar(
                out=stat[:, 12:13], in0=stat[:, 12:13], scalar1=1.0 / D,
                scalar2=None, op0=ALU.mult))
            op("vector", lambda e: e.tensor_scalar(
                out=stat[:, 0:1], in0=stat[:, 0:1], scalar1=1.0 / D,
                scalar2=None, op0=ALU.mult))
            op("vector", lambda e: e.drain())
            op("vector", lambda e: e.tensor_tensor(
                out=stat[:, 2:3], in0=stat[:, 12:13], in1=stat[:, 12:13],
                op=ALU.mult))
            op("vector", lambda e: e.drain())
            mv = op("vector", lambda e: e.tensor_tensor(
                out=stat[:, 2:3], in0=stat[:, 0:1], in1=stat[:, 2:3],
                op=ALU.subtract), inc=True)
            wait("scalar", sV, mv)
            op("scalar", lambda e: e.activation(
                out=stat[:, 3:4], in_=stat[:, 2:3], func=AF.Ln,
                bias=ceps_sb[:, 0:1], scale=1.0))
            ms2 = op("scalar", lambda e: e.activation(
                out=stat[:, 4:5], in_=stat[:, 3:4], func=AF.Exp,
                scale=-0.5), inc=True)
            wait("vector", sS, ms2)
            m_fin = op("vector", lambda e, cch=cch: e.tensor_scalar(
                out=y_sb[:, cch, :], in0=y_sb[:, cch, :],
                scalar1=stat[:, 12:13], scalar2=stat[:, 4:5],
                op0=ALU.subtract, op1=ALU.mult), inc=True)
            wait("gpsimd", sV, m_fin)
            dma_out("gpsimd", y_out[tb + cch * 128:tb + (cch + 1) * 128, :],
                    y_sb[:, cch, :])
        barrier()
    bf.close()

    checkpoint("bffn")
    if stop_phase is not None:
        cut = checkpoints[stop_phase]
        for k in streams:
            streams[k] = streams[k][:cut[k]]
    else:
        streams["gpsimd"].append(lambda e, v=sO.n: e.wait_ge(sO.sem, v))

    with nc.Block() as block:
        block.sync(lambda e, s=streams["sync"]: [f(e) for f in s])
        block.scalar(lambda e, s=streams["scalar"]: [f(e) for f in s])
        block.vector(lambda e, s=streams["vector"]: [f(e) for f in s])
        block.tensor(lambda e, s=streams["tensor"]: [f(e) for f in s])
        block.gpsimd(lambda e, s=streams["gpsimd"]: [f(e) for f in s])

    ctx.close()
    return nc, dbg


_NC_CACHE = {}


def _get_nc(debug=False):
    if debug not in _NC_CACHE:
        _NC_CACHE[debug] = build_program(debug=debug)
    return _NC_CACHE[debug]


def make_in_maps(inputs):
    import ml_dtypes
    x = np.asarray(inputs["L0_attn_x"])[0]
    Wq_ = np.ascontiguousarray(np.asarray(inputs["L0_attn_Wq"], np.float32))
    Wk_ = np.ascontiguousarray(np.asarray(inputs["L0_attn_Wk"], np.float32))
    Wv_ = np.ascontiguousarray(np.asarray(inputs["L0_attn_Wv"], np.float32))
    Wo_ = np.ascontiguousarray(np.asarray(inputs["L0_attn_Wo"], np.float32))
    bo_ = np.asarray(inputs["L0_attn_bo"], np.float32).reshape(1, D)
    ps_ = np.ascontiguousarray(
        np.asarray(inputs["L0_attn_phi_scale"], np.float32)[:, 0, :].T)
    pb_ = np.ascontiguousarray(
        np.asarray(inputs["L0_attn_phi_bias"], np.float32)[:, 0, :].T)
    sigma = np.asarray(inputs["L0_ffn_down_reorder_index"]).astype(np.int64)
    wg_ = np.ascontiguousarray(
        np.asarray(inputs["L0_ffn_w_gate"], np.float32)[sigma])
    wu_ = np.ascontiguousarray(
        np.asarray(inputs["L0_ffn_w_up"], np.float32)[sigma])
    wd_ = np.ascontiguousarray(np.asarray(inputs["L0_ffn_w_down"], np.float32))
    id32 = np.eye(128, dtype=np.float32)
    id8 = np.eye(128, dtype=np.float32).astype(ml_dtypes.float8_e4m3)
    in_maps = []
    for c in range(N_CORES):
        sl = slice(c * TOK, (c + 1) * TOK)
        msk = np.zeros((KD, N_CORES), np.float32)
        msk[:, :c] = 1.0
        in_maps.append({
            "xT": np.ascontiguousarray(x[sl].T.astype(np.float32)),
            "xN": np.ascontiguousarray(x[sl].astype(np.float32)),
            "Wq": Wq_, "Wk": Wk_, "Wv": Wv_, "Wo": Wo_, "bo": bo_,
            "ps": ps_, "pb": pb_, "wg": wg_, "wu": wu_, "wd": wd_,
            "mask": msk, "id32": id32, "id8": id8,
        })
    return in_maps


def kernel(**inputs):
    nc, _ = _get_nc(debug=False)
    in_maps = make_in_maps(inputs)
    res = run_bass_kernel_spmd(nc, in_maps, core_ids=list(range(N_CORES)))
    y = np.concatenate([r["y"] for r in res.results], axis=0)
    return y.reshape(1, T_FULL, D).astype(np.float32)


if __name__ == "__main__":
    nc, _ = build_program(debug=False)
    n = sum(len(b.instructions) if hasattr(b, "instructions") else 0
            for b in []) or "?"
    print("built ok")
